# version 06
# baseline (speedup 1.0000x reference)
import contextlib
import numpy as np
import ml_dtypes
import concourse.bass as bass
import concourse.mybir as mybir

F32 = mybir.dt.float32
BF16 = mybir.dt.bfloat16
AF = mybir.ActivationFunctionType
ALU = mybir.AluOpType
AX = mybir.AxisListType

D = 1024
T = 2048
NT = T // 128
NG = T // 512
NE = 8
LN8INV = float(np.log(0.125))
EPS = 1e-5
ENGS = "tvsgy"


class Step:
    __slots__ = ("eng", "fn", "deps", "needed", "ord", "name")

    def __init__(self, eng, fn, deps, name):
        self.eng = eng
        self.fn = fn
        self.deps = deps
        self.needed = False
        self.ord = None
        self.name = name


class DmaTok:
    __slots__ = ("sem_id", "val")

    def __init__(self, sem_id, val):
        self.sem_id = sem_id
        self.val = val


class Chan:

    def __init__(self, b, eng, nsems, tag):
        self.b = b
        self.eng = eng
        self.nsems = nsems
        self.sem_ids = [b.new_sem(f"{tag}{i}") for i in range(nsems)]
        self.count = [0] * nsems
        self.rr = 0

    def dma(self, fn, reads=(), writes=(), deps=(), name="dma"):
        i = self.rr % self.nsems
        self.rr += 1
        deps = list(deps)
        if self.count[i] > 0:
            deps.append(DmaTok(self.sem_ids[i], 16 * self.count[i]))
        self.count[i] += 1
        tok = DmaTok(self.sem_ids[i], 16 * self.count[i])
        sem_id = self.sem_ids[i]
        bb = self.b

        def wrap(eng, _fn=fn, _sem_id=sem_id):
            ins = _fn(eng)
            ins.then_inc(bb.sem_handles[_sem_id], 16)
            return None

        self.b.step(self.eng, wrap, reads=reads, writes=writes, deps=deps,
                    name=name, write_tok=tok)
        return tok


class Builder:
    def __init__(self, nc):
        self.nc = nc
        self.steps = []
        self.eng_steps = {e: [] for e in ENGS}
        self.res = {}
        self.sem_names = []
        self.sem_handles = []
        self.eng_last = {}
        self._semstack = contextlib.ExitStack()
        for e in ENGS:
            self.new_sem(f"se_{e}")

    def new_sem(self, name):
        sid = len(self.sem_names)
        self.sem_names.append(name)
        h = self._semstack.enter_context(self.nc.semaphore(name))
        self.sem_handles.append(h)
        return sid

    def step(self, eng, fn, reads=(), writes=(), deps=(), name="",
             write_tok=None):
        dl = [d for d in deps if d is not None]
        for r in reads:
            e = self.res.get(r)
            if e and e["w"] is not None:
                dl.append(e["w"])
        for w in writes:
            e = self.res.get(w)
            if e:
                if e["w"] is not None:
                    dl.append(e["w"])
                dl.extend(e["r"])
        st = Step(eng, fn, dl, name)
        self.steps.append(st)
        self.eng_steps[eng].append(st)
        tok = write_tok if write_tok is not None else st
        self.eng_last[eng] = tok
        for r in reads:
            self.res.setdefault(r, {"w": None, "r": []})["r"].append(tok)
        for w in writes:
            self.res[w] = {"w": tok, "r": []}
        return tok

    def barrier(self, name="bar"):
        lasts = {e: self.eng_last[e] for e in ENGS if e in self.eng_last}
        for e in ENGS:
            deps = [s for e2, s in lasts.items() if e2 != e]
            self.step(e, lambda eng: eng.nop(), deps=deps, name=f"{name}_{e}")

    def resolve(self):
        for st in self.steps:
            for d in st.deps:
                if isinstance(d, Step) and d.eng != st.eng:
                    d.needed = True
        for e in ENGS:
            c = 0
            for st in self.eng_steps[e]:
                if st.needed:
                    c += 1
                    st.ord = c

    def emit(self, eng_obj, e):
        hw = {}
        for st in self.eng_steps[e]:
            waits = {}
            for d in st.deps:
                if isinstance(d, DmaTok):
                    sid, val = d.sem_id, d.val
                else:
                    if d.eng == e:
                        continue
                    assert d.ord is not None, f"dep of {st.name} unmarked"
                    sid, val = ENGS.index(d.eng), d.ord
                if val > waits.get(sid, 0):
                    waits[sid] = val
            for sid, val in sorted(waits.items()):
                if val > hw.get(sid, 0):
                    eng_obj.wait_ge(self.sem_handles[sid], val)
                    hw[sid] = val
            ins = st.fn(eng_obj)
            if st.needed:
                assert ins is not None, f"step {st.name} marked, fn gave None"
                if e in ("v", "s"):
                    ins = eng_obj.drain()
                ins.then_inc(self.sem_handles[ENGS.index(e)], 1)


def ts(i, n):
    return slice(i * n, (i + 1) * n)


def sl(o, n):
    return slice(o, o + n)


def build_core_program(dbg=(), phases="BCD"):
    nc = bass.Bass("TRN2", target_bir_lowering=False)
    dbg = set(dbg)
    dbg_done = []

    dx = nc.dram_tensor("x", [T, D], F32, kind="ExternalInput").ap()
    dxprev = nc.dram_tensor("xprev", [T, D], F32, kind="ExternalInput").ap()
    dWq = nc.dram_tensor("Wq", [D, 512], F32, kind="ExternalInput").ap()
    dWk = nc.dram_tensor("Wk", [D, 512], F32, kind="ExternalInput").ap()
    dWv = nc.dram_tensor("Wv", [D, D], F32, kind="ExternalInput").ap()
    dWg = nc.dram_tensor("Wg", [D, D], F32, kind="ExternalInput").ap()
    dWgk1 = nc.dram_tensor("Wgk1", [D, 16], F32, kind="ExternalInput").ap()
    dWgk2 = nc.dram_tensor("Wgk2", [16, 512], F32, kind="ExternalInput").ap()
    dbgk2t = nc.dram_tensor("bgk2t", [128, 4], F32, kind="ExternalInput").ap()
    dbgk2r = nc.dram_tensor("bgk2r", [1, 512], F32, kind="ExternalInput").ap()
    dnormw = nc.dram_tensor("normw", [128, 1], F32, kind="ExternalInput").ap()
    dWo = nc.dram_tensor("Wo", [D, D], F32, kind="ExternalInput").ap()
    dWr = nc.dram_tensor("Wr", [D, 8], F32, kind="ExternalInput").ap()
    dbr = nc.dram_tensor("br", [1, 8], F32, kind="ExternalInput").ap()
    dW1 = nc.dram_tensor("W1", [NE, D, 2048], F32, kind="ExternalInput").ap()
    db1t = nc.dram_tensor("b1t", [128, 128], F32, kind="ExternalInput").ap()
    dW2 = nc.dram_tensor("W2", [NE, 2048, D], F32, kind="ExternalInput").ap()
    db2 = nc.dram_tensor("b2", [NE, D], F32, kind="ExternalInput").ap()
    dy = nc.dram_tensor("y", [T, D], F32, kind="ExternalOutput").ap()
    dx2d = nc.dram_tensor("x2d", [T, D], F32, kind="Internal").ap()
    dxn2d = nc.dram_tensor("xn2d", [128, 8, T], BF16, kind="Internal").ap()

    iden = np.eye(128, dtype=np.float32)
    triu = np.triu(np.ones((64, 64), dtype=np.float32))
    lbd_np = np.zeros((128, 128), dtype=np.float32)
    lbd_np[0:64, 0:64] = triu
    lbd_np[64:128, 64:128] = triu
    d_identf = nc.inline_tensor(iden, "identf_d").ap()
    d_identb = nc.inline_tensor(iden.astype(ml_dtypes.bfloat16), "identb_d").ap()
    d_mask8 = nc.inline_tensor(
        np.tile(triu, (2, 8)).astype(ml_dtypes.bfloat16), "mask8_d").ap()
    d_lbd = nc.inline_tensor(lbd_np.astype(ml_dtypes.bfloat16), "lbd_d").ap()
    d_onesf = nc.inline_tensor(np.ones((1, 128), dtype=np.float32), "onesf_d").ap()
    CONSTS = [EPS, 1.0, LN8INV, 2.0, 0.0]
    d_constp = nc.inline_tensor(
        np.tile(np.asarray(CONSTS, dtype=np.float32)[None, :], (128, 1)),
        "constp_d").ap()

    b = Builder(nc)
    st_p = contextlib.ExitStack()
    st_a = contextlib.ExitStack()
    st_f = contextlib.ExitStack()

    def sb(stk, name, shape, dt):
        return stk.enter_context(nc.sbuf_tensor(name, shape, dt))

    PB = [st_p.enter_context(nc.psum_tensor(f"pb{i}", [128, 512], F32))
          for i in range(6)]
    PT = st_p.enter_context(nc.psum_tensor("pt", [128, 1024], BF16))
    PR = st_p.enter_context(nc.psum_tensor("pr", [128, 512], F32))

    identf = sb(st_p, "identf", [128, 128], F32)
    identb = sb(st_p, "identb", [128, 128], BF16)
    mask8 = sb(st_p, "mask8", [128, 512], BF16)
    lbd = sb(st_p, "lbd", [128, 128], BF16)
    onesf = sb(st_p, "onesf", [1, 128], F32)
    constp = sb(st_p, "constp", [128, 8], F32)
    for i, v in enumerate(CONSTS):
        nc.const_aps.register(F32, float(v), constp[:, i:i + 1])
    S_sb = sb(st_p, "S_sb", [128, 512], F32)
    shist = sb(st_p, "shist", [128, 4608], BF16)
    cw_sb = sb(st_p, "cw_sb", [128, 128], F32)
    cwf = sb(st_p, "cwf", [8, 2048], BF16)
    b1t = sb(st_p, "b1t_s", [128, 128], F32)
    b2sb = sb(st_p, "b2sb", [8, 1024], BF16)
    brbc = sb(st_p, "brbc", [128, 8], F32)
    brrow = sb(st_p, "brrow", [1, 8], F32)
    wr_sb = sb(st_p, "wr", [128, 64], BF16)
    normw = sb(st_p, "normw_s", [128, 1], F32)
    bgk2t = sb(st_p, "bgk2t_s", [128, 4], F32)
    nbgk2t = sb(st_p, "nbgk2t", [128, 4], F32)
    bgk2r = sb(st_p, "bgk2r_s", [1, 512], F32)
    bgk2bc = sb(st_p, "bgk2bc", [128, 512], F32)
    sumx = [sb(st_p, f"sumx{i}", [128, 1], F32) for i in range(2)]
    sumsq = [sb(st_p, f"sumsq{i}", [128, 1], F32) for i in range(2)]
    meanv = [sb(st_p, f"meanv{i}", [128, 1], F32) for i in range(2)]
    ex2 = [sb(st_p, f"ex2_{i}", [128, 1], F32) for i in range(2)]
    negvar = [sb(st_p, f"negvar{i}", [128, 1], F32) for i in range(2)]
    lnv = [sb(st_p, f"lnv{i}", [128, 1], F32) for i in range(2)]
    rstd = [sb(st_p, f"rstd{i}", [128, 1], F32) for i in range(2)]
    nbias = [sb(st_p, f"nbias{i}", [128, 1], F32) for i in range(2)]
    m1 = sb(st_p, "m1", [128, 1], F32)
    negm1 = sb(st_p, "negm1", [128, 1], F32)
    zden = sb(st_p, "zden", [128, 1], F32)
    rz = sb(st_p, "rz", [128, 1], F32)
    logits = sb(st_p, "logits", [128, 8], F32)
    pex = sb(st_p, "pex", [128, 8], F32)
    probs = sb(st_p, "probs", [128, 8], F32)
    rk = sb(st_p, "rk", [128, 8], F32)
    keep = sb(st_p, "keep", [128, 8], F32)
    gtb = sb(st_p, "gtb", [128, 64], F32)
    eqb = sb(st_p, "eqb", [128, 64], F32)
    cnt8 = sb(st_p, "cnt8", [128, 8], F32)

    chG = Chan(b, "g", 6, "cg")
    chY = Chan(b, "y", 4, "cy")
    chS = Chan(b, "y", 4, "cs")

    def dbg_dump(name, ap, shape, dt, reads):
        if name not in dbg:
            return
        d = nc.dram_tensor("dbg_" + name, list(shape), dt,
                           kind="ExternalOutput").ap()
        dbg_done.append(name)
        chS.dma(lambda e, _d=d, _a=ap: e.dma_start(_d, _a),
                reads=reads, name=f"dbg_{name}")

    chG.dma(lambda e: e.dma_start(identf[:], d_identf), writes=["identf"])
    chG.dma(lambda e: e.dma_start(identb[:], d_identb), writes=["identb"])
    chG.dma(lambda e: e.dma_start(mask8[:], d_mask8), writes=["mask8"])
    chG.dma(lambda e: e.dma_start(lbd[:], d_lbd), writes=["lbd"])
    chG.dma(lambda e: e.dma_start(onesf[:], d_onesf), writes=["onesf"])
    chG.dma(lambda e: e.dma_start(constp[:, 0:len(CONSTS)], d_constp),
            writes=["constp"])
    chG.dma(lambda e: e.dma_start(bgk2t[:], dbgk2t), writes=["bgk2t"])
    chG.dma(lambda e: e.dma_start(bgk2r[:], dbgk2r), writes=["bgk2r"])
    chG.dma(lambda e: e.dma_start(normw[:], dnormw), writes=["normw"])
    chG.dma(lambda e: e.dma_start(brrow[:], dbr), writes=["brrow"])
    chG.dma(lambda e: e.dma_start(b1t[:], db1t), writes=["b1t"])
    chG.dma(lambda e: e.dma_start(b2sb[:], db2), writes=["b2sb"])

    wq = sb(st_a, "wq", [128, 4096], BF16)
    wk = sb(st_a, "wk", [128, 4096], BF16)
    wv = sb(st_a, "wv", [128, 8192], BF16)
    wg = sb(st_a, "wg", [128, 8192], BF16)
    wo = sb(st_a, "wo", [128, 8192], BF16)
    wgk1 = sb(st_a, "wgk1", [128, 128], BF16)
    wgk2 = sb(st_a, "wgk2", [16, 512], BF16)

    for k in range(8):
        chG.dma(lambda e, k=k: e.dma_start(wq[:, ts(k, 512)], dWq[ts(k, 128), :]),
                writes=[f"wq{k}"], name=f"ldwq{k}")
        chG.dma(lambda e, k=k: e.dma_start(wk[:, ts(k, 512)], dWk[ts(k, 128), :]),
                writes=[f"wk{k}"], name=f"ldwk{k}")
        chG.dma(lambda e, k=k: e.dma_start(wv[:, ts(k, 1024)], dWv[ts(k, 128), :]),
                writes=[f"wv{k}"], name=f"ldwv{k}")
        chG.dma(lambda e, k=k: e.dma_start(wg[:, ts(k, 1024)], dWg[ts(k, 128), :]),
                writes=[f"wg{k}"], name=f"ldwg{k}")
        chG.dma(lambda e, k=k: e.dma_start(wo[:, ts(k, 1024)], dWo[ts(k, 128), :]),
                writes=[f"wo{k}"], name=f"ldwo{k}")
        chG.dma(lambda e, k=k: e.dma_start(wgk1[:, ts(k, 16)], dWgk1[ts(k, 128), :]),
                writes=[f"wgk1_{k}"], name=f"ldwgk1{k}")
        chG.dma(lambda e, k=k: e.dma_start(wr_sb[:, ts(k, 8)], dWr[ts(k, 128), :]),
                writes=[f"wr{k}"], name=f"ldwr{k}")
    chG.dma(lambda e: e.dma_start(wgk2[:], dWgk2), writes=["wgk2"])

    b.step("v", lambda e: e.tensor_scalar(
        nbgk2t[:], bgk2t[:], -1.0, None, ALU.mult),
        reads=["bgk2t"], writes=["nbgk2t"], name="nbgk2t")
    b.step("t", lambda e: e.matmul(PR[:, 0:8], onesf[0:1, :], brrow[:],
                                   start=True, stop=True),
           reads=["onesf", "brrow"], writes=["PR"], name="brmm")
    b.step("v", lambda e: e.tensor_copy(brbc[:], PR[:, 0:8]),
           reads=["PR"], writes=["brbc"], name="brbcv")
    b.step("t", lambda e: e.matmul(PR[:, :], onesf[0:1, :], bgk2r[:],
                                   start=True, stop=True),
           reads=["onesf", "bgk2r"], writes=["PR"], name="bgmm")
    b.step("v", lambda e: e.tensor_copy(bgk2bc[:], PR[:, :]),
           reads=["PR"], writes=["bgk2bc"], name="bgbcv")
    b.step("v", lambda e: e.memset(S_sb[:], 0.0), writes=["S"], name="s0")
    for k in range(8):
        b.step("v", lambda e, k=k: e.tensor_scalar(
            wo[:, ts(k, 1024)], wo[:, ts(k, 1024)], normw[:], None, ALU.mult),
            reads=[f"wo{k}", "normw"], writes=[f"wo{k}"], name=f"wosc{k}")

    xin = [sb(st_a, f"xin{i}", [128, 1024], F32) for i in range(2)]
    junkS = sb(st_a, "junkS", [128, 1024], BF16)
    xnt = [sb(st_a, f"xnt{i}", [128, 1024], BF16) for i in range(3)]
    xnfb = [sb(st_a, f"xnf{i}", [128, 4096], BF16) for i in range(2)]
    z1f = sb(st_a, "z1f", [16, 512], BF16)
    uf1 = [sb(st_a, f"uf1_{i}", [128, 512], BF16) for i in range(2)]
    spf = sb(st_a, "spf", [128, 2048], F32)
    facq = [sb(st_a, "facq0", [128, 512], F32)] * 2
    qe = sb(st_a, "qe", [128, 2048], BF16)
    ke = sb(st_a, "ke", [128, 2048], BF16)
    vcg = sb(st_a, "vcg", [128, 4096], BF16)
    gcb = [sb(st_a, f"gc{i}", [128, 1024], BF16) for i in range(2)]
    ktb = [sb(st_a, f"ktb{i}", [128, 512], BF16) for i in range(2)]
    utb = [sb(st_a, f"utb{i}", [128, 512], BF16) for i in range(3)]
    ketb = [sb(st_a, f"ket{i}", [128, 512], BF16) for i in range(2)]
    zb = sb(st_a, "zb", [128, 512], BF16)
    kef = sb(st_a, "kef", [128, 512], BF16)
    decb = [sb(st_a, f"dec{i}", [128, 4], F32) for i in range(2)]
    amtb = [sb(st_a, f"amt{i}", [128, 512], BF16) for i in range(2)]
    o_sb = [sb(st_a, f"o_sb{i}", [128, 1024], BF16) for i in range(2)]
    osq = sb(st_a, "osq", [128, 1024], BF16)
    sil2 = sb(st_a, "sil2", [128, 1024], BF16)
    sums8 = sb(st_a, "sums8", [128, 8], F32)
    lnr8 = sb(st_a, "lnr8", [128, 8], F32)
    rmsr = sb(st_a, "rmsr", [128, 8], F32)
    ogc = [sb(st_a, f"ogc{i}", [128, 1024], BF16) for i in range(3)]
    ogf = sb(st_a, "ogf", [128, 4096], BF16)
    x2t = [sb(st_a, f"x2t{i}", [128, 1024], F32) for i in range(2)]
    xn2t = [sb(st_a, "xn2t0", [128, 1024], BF16)] * 2
    xn2f = sb(st_a, "xn2f", [128, 4096], BF16)

    wq_deps = [f"wq{k}" for k in range(8)]
    wk_deps = [f"wk{k}" for k in range(8)]
    wv_deps = [f"wv{k}" for k in range(8)]
    wg_deps = [f"wg{k}" for k in range(8)]
    wo_deps = [f"wo{k}" for k in range(8)]
    wgk1_deps = [f"wgk1_{k}" for k in range(8)]
    wr_deps = [f"wr{k}" for k in range(8)]

    def ln_tile(src_dram, t_glob, buf_i, out_xnt, out_key, tag):
        xk = f"xin{buf_i}"
        p = buf_i % 2
        chY.dma(lambda e: e.dma_start(xin[buf_i][:], src_dram[ts(t_glob, 128), :]),
                writes=[xk], name=f"ldx_{tag}")
        b.step("v", lambda e: e.reduce_sum(sumx[p][:], xin[buf_i][:], AX.X),
               reads=[xk], writes=[f"sumx{p}"], name=f"lns_{tag}")
        b.step("s", lambda e: e.activation(junkS[:], xin[buf_i][:], AF.Square,
                                           accum_out=sumsq[p][:]),
               reads=[xk, "constp"], writes=[f"sumsq{p}", "junkS"],
               name=f"lnq_{tag}")

        def f_stat(e):
            e.tensor_scalar(meanv[p][:], sumx[p][:], 1.0 / D, None, ALU.mult)
            e.tensor_scalar(ex2[p][:], sumsq[p][:], 1.0 / D, None, ALU.mult)
            e.drain()
            return e.scalar_tensor_tensor(negvar[p][:], meanv[p][:],
                                          meanv[p][:], ex2[p][:],
                                          ALU.mult, ALU.subtract)
        b.step("v", f_stat, reads=[f"sumx{p}", f"sumsq{p}"],
               writes=[f"meanv{p}", f"negvar{p}", f"ex2_{p}"],
               name=f"lnst_{tag}")

        def f_rstd(e):
            e.activation(lnv[p][:], negvar[p][:], AF.Ln, bias=EPS, scale=-1.0)
            e.drain()
            return e.activation(rstd[p][:], lnv[p][:], AF.Exp, scale=-0.5)
        b.step("s", f_rstd, reads=[f"negvar{p}", "constp"],
               writes=[f"rstd{p}", f"lnv{p}"], name=f"lnr_{tag}")
        b.step("v", lambda e: e.tensor_scalar(nbias[p][:], meanv[p][:],
                                              rstd[p][:], -1.0,
                                              ALU.mult, ALU.mult),
               reads=[f"meanv{p}", f"rstd{p}"], writes=[f"nbias{p}"],
               name=f"lnb_{tag}")
        b.step("s", lambda e: e.activation(out_xnt[:], xin[buf_i][:],
                                           AF.Identity, bias=nbias[p][:],
                                           scale=rstd[p][:]),
               reads=[xk, f"rstd{p}", f"nbias{p}"], writes=[out_key],
               name=f"lna_{tag}")

    def transpose_tile(src, src_key, dst, dst_key, tl, tag):
        def f_tr(e):
            r = None
            for k in range(8):
                r = e.transpose(PT[:, ts(k, 128)], src[:, ts(k, 128)],
                                identb[:])
            return r
        b.step("t", f_tr, reads=[src_key, "identb"], writes=["PT"],
               name=f"tr_{tag}")
        dst3 = dst[:].rearrange("p (k c) -> p k c", k=8, c=512)[:, :, ts(tl, 128)]
        src3 = PT[:].rearrange("p (k c) -> p k c", k=8, c=128)
        b.step("v", lambda e: e.tensor_copy(dst3, src3),
               reads=["PT"], writes=[dst_key], name=f"tre_{tag}")

    def attn_ln_pre(gi, src_dram, ph, tl):
        t_glob = gi * 4 + tl
        tag = f"{ph}{gi}t{tl}"
        ln_tile(src_dram, t_glob, tl % 2, xnt[tl % 3], f"xnt{tl%3}", tag)

    def attn_ln_tr(gi, ph, tl):
        tag = f"{ph}{gi}t{tl}"
        transpose_tile(xnt[tl % 3], f"xnt{tl%3}", xnfb[gi % 2], f"xnf{gi%2}",
                       tl, tag)

    def attn_group(gi, src_dram, full, ph, cut=None, prefetch=None):
        tag0 = f"{ph}{gi}"
        xnf = xnfb[gi % 2]
        xnfk = f"xnf{gi%2}"

        def f_z1(e):
            r = None
            for k in range(8):
                r = e.matmul(PR[0:16, :], wgk1[:, ts(k, 16)], xnf[:, ts(k, 512)],
                             start=(k == 0), stop=(k == 7))
            return r
        b.step("t", f_z1, reads=[xnfk] + wgk1_deps, writes=["PR"],
               name=f"z1_{tag0}")
        b.step("v", lambda e: e.tensor_copy(z1f[:], PR[0:16, :]),
               reads=["PR"], writes=["z1f"], name=f"z1e_{tag0}")

        if full:
            for pc in range(4):
                b.step("t", lambda e, pc=pc: e.matmul(
                    PB[pc % 2][:, :], wgk2[:, ts(pc, 128)], z1f[:],
                    start=True, stop=True),
                    reads=["z1f", "wgk2"], writes=[f"PB{pc%2}"],
                    name=f"zf_{tag0}p{pc}")

                def f_uf(e, pc=pc):
                    e.activation(zb[:], PB[pc % 2][:, :], AF.Exp,
                                 bias=nbgk2t[:, pc:pc + 1], scale=-1.0)
                    e.drain()
                    return e.activation(uf1[pc % 2][:], zb[:], AF.Ln, bias=1.0)
                b.step("s", f_uf, reads=[f"PB{pc%2}", "nbgk2t", "constp"],
                       writes=[f"uf{pc%2}", "zbuf"], name=f"uf_{tag0}p{pc}")

                def f_spf(e, pc=pc):
                    r = None
                    for c8 in range(8):
                        r = e.tensor_tensor_scan(
                            spf[:, sl(pc * 512 + c8 * 64, 64)],
                            uf1[pc % 2][:, ts(c8, 64)],
                            uf1[pc % 2][:, ts(c8, 64)],
                            0.0, ALU.add, ALU.bypass)
                    return r
                b.step("v", f_spf, reads=[f"uf{pc%2}", "constp"], writes=["spf"],
                       name=f"spf_{tag0}p{pc}")
            for pc in range(4):
                def f_qp(e, pc=pc):
                    r = None
                    for k in range(8):
                        r = e.matmul(PB[2 + pc % 2][:, :],
                                     wq[:, sl(k * 512 + pc * 128, 128)],
                                     xnf[:, ts(k, 512)],
                                     start=(k == 0), stop=(k == 7))
                    return r
                b.step("t", f_qp, reads=[xnfk] + wq_deps,
                       writes=[f"PB{2+pc%2}"], name=f"qp_{tag0}p{pc}")
                b.step("s", lambda e, pc=pc: e.activation(
                    facq[pc % 2][:], spf[:, ts(pc, 512)], AF.Exp,
                    scale=-1.0 / 16, bias=LN8INV),
                    reads=["spf", "constp"], writes=["facq0"],
                    name=f"qf_{tag0}p{pc}")
                b.step("v", lambda e, pc=pc: e.tensor_tensor(
                    qe[:, ts(pc, 512)], PB[2 + pc % 2][:, :],
                    facq[pc % 2][:], ALU.mult),
                    reads=[f"PB{2+pc%2}", "facq0"], writes=["qe"],
                    name=f"qe_{tag0}p{pc}")
            for pc in range(4):
                def f_kp(e, pc=pc):
                    r = None
                    for k in range(8):
                        r = e.matmul(PB[4 + pc % 2][:, :],
                                     wk[:, sl(k * 512 + pc * 128, 128)],
                                     xnf[:, ts(k, 512)],
                                     start=(k == 0), stop=(k == 7))
                    return r
                b.step("t", f_kp, reads=[xnfk] + wk_deps,
                       writes=[f"PB{4+pc%2}"], name=f"kp_{tag0}p{pc}")
                b.step("s", lambda e, pc=pc: e.activation(
                    facq[pc % 2][:], spf[:, ts(pc, 512)], AF.Exp,
                    scale=1.0 / 16),
                    reads=["spf", "facq0"], writes=["facq0"],
                    name=f"kf_{tag0}p{pc}")
                b.step("v", lambda e, pc=pc: e.tensor_tensor(
                    ke[:, ts(pc, 512)], PB[4 + pc % 2][:, :],
                    facq[pc % 2][:], ALU.mult),
                    reads=[f"PB{4+pc%2}", "facq0"], writes=["ke"],
                    name=f"ke_{tag0}p{pc}")

        if cut == "f":
            return
        def pair_ops(tl):
            tag = f"{tag0}P{tl}"
            tok0 = tl * 128

            def f_vp(e, tok0=tok0):
                r = None
                for dc in range(2):
                    for k in range(8):
                        r = e.matmul(PB[dc][:, :],
                                     xnf[:, sl(k * 512 + tok0, 128)],
                                     wv[:, sl(k * 1024 + dc * 512, 512)],
                                     start=(k == 0), stop=(k == 7))
                return r
            b.step("t", f_vp, reads=[xnfk] + wv_deps, writes=["PB0", "PB1"],
                   name=f"vp_{tag}")

            def f_ve(e, tl=tl):
                e.tensor_copy(vcg[:, sl(tl * 1024, 512)], PB[0][:, :])
                return e.tensor_copy(vcg[:, sl(tl * 1024 + 512, 512)],
                                     PB[1][:, :])
            b.step("v", f_ve, reads=["PB0", "PB1"], writes=[f"vc{tl}"],
                   name=f"ve_{tag}")

            def f_kt(e, tok0=tok0):
                r = None
                for k in range(8):
                    r = e.matmul(PB[2][:, :], xnf[:, sl(k * 512 + tok0, 128)],
                                 wk[:, ts(k, 512)],
                                 start=(k == 0), stop=(k == 7))
                return r
            b.step("t", f_kt, reads=[xnfk] + wk_deps, writes=["PB2"],
                   name=f"kt_{tag}")
            b.step("v", lambda e, tl=tl: e.tensor_copy(ktb[tl % 2][:],
                                                       PB[2][:, :]),
                   reads=["PB2"], writes=[f"ktb{tl%2}"], name=f"kte_{tag}")

            b.step("t", lambda e, tok0=tok0: e.matmul(
                PB[5][:, :], z1f[0:16, sl(tok0, 128)], wgk2[:],
                start=True, stop=True),
                reads=["z1f", "wgk2"], writes=["PB5"], name=f"zt_{tag}")
            b.step("v", lambda e: e.tensor_tensor(zb[:], PB[5][:, :],
                                                  bgk2bc[:], ALU.add),
                   reads=["PB5", "bgk2bc"], writes=["zbuf"], name=f"ub_{tag}")

            def f_ut(e, tl=tl):
                e.activation(zb[:], zb[:], AF.Exp, scale=-1.0)
                e.drain()
                return e.activation(utb[tl % 3][:], zb[:], AF.Ln, bias=1.0)
            b.step("s", f_ut, reads=["zbuf", "constp"],
                   writes=[f"ut{tl%3}", "zbuf"], name=f"ut_{tag}")

        def pair_ops2(tl):
            tag = f"{tag0}Q{tl}"
            b.step("t", lambda e, tl=tl: e.matmul(
                PB[3][:, :], lbd[:], utb[tl % 3][:], start=True, stop=True),
                reads=[f"ut{tl%3}", "lbd"], writes=["PB3"], name=f"spt_{tag}")
            b.step("s", lambda e: e.activation(kef[:], PB[3][:, :], AF.Exp,
                                               scale=1.0 / 16),
                   reads=["PB3"], writes=["kef"], name=f"kef_{tag}")
            b.step("v", lambda e, tl=tl: e.tensor_tensor(
                ketb[tl % 2][:], ktb[tl % 2][:], kef[:], ALU.mult),
                reads=[f"ktb{tl%2}", "kef"], writes=[f"ket{tl%2}"],
                name=f"ket_{tag}")

        def chunk_ops(c):
            tag = f"{tag0}c{c}"
            tl = c // 2
            cp = c % 2
            slot = (gi * 8 + c) % 9
            if full:
                spf3 = spf[:].rearrange("p (h c2) -> p h c2", h=4, c2=512)
                b.step("s", lambda e, c=c, spf3=spf3: e.activation(
                    decb[c % 2][:], spf3[:, :, sl(c * 64 + 63, 1)], AF.Exp,
                    scale=-1.0 / 16),
                    reads=["spf"], writes=[f"dec{c%2}"], name=f"dec_{tag}")
            else:
                def f_decmm(e, tl=tl, cp=cp):
                    r = None
                    for h in range(8):
                        r = e.matmul(PR[sl((h % 2) * 64, 64), sl(h // 2, 1)],
                                     utb[tl % 3][sl(cp * 64, 64), ts(h, 64)],
                                     mask8[sl(cp * 64, 64), 63:64],
                                     start=True, stop=True,
                                     tile_position=(cp * 64, (h % 2) * 64))
                    return r
                b.step("t", f_decmm, reads=[f"ut{tl%3}", "mask8"], writes=["PR"],
                       name=f"decm_{tag}")
                b.step("s", lambda e, c=c: e.activation(
                    decb[c % 2][:], PR[:, 0:4], AF.Exp, scale=-1.0 / 16),
                    reads=["PR"], writes=[f"dec{c%2}"], name=f"dec_{tag}")

            m3b = 4 if cp == 0 else 3

            def f_mm3(e, tl=tl, cp=cp, m3b=m3b):
                r = None
                for h in range(8):
                    r = e.matmul(PB[m3b][sl((h % 2) * 64, 64), ts(h // 2, 128)],
                                 ketb[tl % 2][sl(cp * 64, 64), ts(h, 64)],
                                 vcg[sl(cp * 64, 64), sl(tl * 1024 + h * 128, 128)],
                                 start=True, stop=True,
                                 tile_position=(cp * 64, (h % 2) * 64))
                return r
            b.step("t", f_mm3, reads=[f"ket{tl%2}", f"vc{tl}"],
                   writes=[f"PB{m3b}"], name=f"mm3_{tag}")

            def f_supd(e, c=c, slot=slot, m3b=m3b):
                e.tensor_tensor(S_sb[:], S_sb[:], PB[m3b][:, :], ALU.add)
                e.drain()
                for pc in range(4):
                    e.tensor_scalar(S_sb[:, ts(pc, 128)], S_sb[:, ts(pc, 128)],
                                    decb[c % 2][:, pc:pc + 1], None, ALU.mult)
                if full:
                    e.drain()
                    return e.tensor_copy(shist[:, ts(slot, 512)], S_sb[:])
                return e.engine_nop()
            b.step("v", f_supd, reads=[f"PB{m3b}", f"dec{c%2}", "S"],
                   writes=(["S", f"sh{slot}"] if full else ["S"]),
                   name=f"supd_{tag}")

        for tl in range(4):
            pair_ops(tl)
            if prefetch is not None:
                prefetch(0, tl)
            if tl >= 1:
                pair_ops2(tl - 1)
            if tl >= 2:
                chunk_ops(2 * (tl - 2))
                chunk_ops(2 * (tl - 2) + 1)
                if prefetch is not None:
                    prefetch(1, tl - 2)
        pair_ops2(3)
        chunk_ops(4)
        chunk_ops(5)
        if prefetch is not None:
            prefetch(1, 2)
        chunk_ops(6)
        chunk_ops(7)
        if prefetch is not None:
            prefetch(1, 3)

        if not full:
            return
        if cut == "s":
            return

        def o_pair(tl):
            tag = f"{tag0}G{tl}"
            tok0 = tl * 128

            def f_gp(e, tok0=tok0):
                r = None
                for dc in range(2):
                    for k in range(8):
                        r = e.matmul(PB[dc][:, :],
                                     xnf[:, sl(k * 512 + tok0, 128)],
                                     wg[:, sl(k * 1024 + dc * 512, 512)],
                                     start=(k == 0), stop=(k == 7))
                return r
            b.step("t", f_gp, reads=[xnfk] + wg_deps, writes=["PB0", "PB1"],
                   name=f"gp_{tag}")

            def f_ge(e, tl=tl):
                e.tensor_copy(gcb[tl % 2][:, 0:512], PB[0][:, :])
                return e.tensor_copy(gcb[tl % 2][:, 512:1024], PB[1][:, :])
            b.step("v", f_ge, reads=["PB0", "PB1"], writes=[f"gc{tl%2}"],
                   name=f"ge_{tag}")

        def o_mm1(c):
            tag = f"{tag0}o{c}"
            tl = c // 2
            cp = c % 2
            m1b = PR if cp == 0 else PB[1]
            m1k = "PR" if cp == 0 else "PB1"

            def dummy_mm(e, row, col):
                e.matmul(m1b[sl(col, 4), 0:4], mask8[sl(row, 64), 0:4],
                         mask8[sl(row, 64), 0:4], start=True, stop=True,
                         tile_position=(row, col))

            def f_mm1(e, c=c, cp=cp, m1b=m1b):
                r = None
                for par in range(2):
                    if par == 1:
                        dummy_mm(e, 0, (1 - cp) * 64)
                    for h in range(par, 8, 2):
                        r = e.matmul(m1b[sl(cp * 64, 64), ts(h, 64)],
                                     ke[sl(par * 64, 64),
                                        sl((h // 2) * 512 + c * 64, 64)],
                                     qe[sl(par * 64, 64),
                                        sl((h // 2) * 512 + c * 64, 64)],
                                     start=True, stop=True,
                                     tile_position=(par * 64, cp * 64))
                return r
            b.step("t", f_mm1, reads=["ke", "qe", "mask8"], writes=[m1k],
                   name=f"mm1_{tag}")
            b.step("v", lambda e, c=c, cp=cp, m1b=m1b: e.tensor_tensor(
                amtb[c % 2][sl(cp * 64, 64), :], m1b[sl(cp * 64, 64), :],
                mask8[sl(cp * 64, 64), :], ALU.mult),
                reads=[m1k, "mask8"], writes=[f"amt{c%2}"], name=f"am_{tag}")

        def o_mm2(c):
            tag = f"{tag0}o{c}"
            tl = c // 2
            cp = c % 2
            prev = (gi * 8 + c - 1) % 9

            def dummy_mm(e, row, col):
                e.matmul(PR[sl(col, 4), 0:4], mask8[sl(row, 64), 0:4],
                         mask8[sl(row, 64), 0:4], start=True, stop=True,
                         tile_position=(row, col))

            def f_mm2(e, c=c, tl=tl, cp=cp, prev=prev):
                r = None
                for h in range(8):
                    r = e.matmul(PB[2 + h // 4][sl(cp * 64, 64), ts(h % 4, 128)],
                                 amtb[c % 2][sl(cp * 64, 64), ts(h, 64)],
                                 vcg[sl(cp * 64, 64),
                                     sl(tl * 1024 + h * 128, 128)],
                                 start=True, stop=True,
                                 tile_position=(cp * 64, cp * 64))
                for par in range(2):
                    if (par == 0 and cp == 1) or par == 1:
                        dummy_mm(e, (1 - par) * 64 if par == 1 else 64,
                                 (1 - cp) * 64)
                    for h in range(par, 8, 2):
                        r = e.matmul(PB[4 + h // 4][sl(cp * 64, 64),
                                                    ts(h % 4, 128)],
                                     qe[sl(par * 64, 64),
                                        sl((h // 2) * 512 + c * 64, 64)],
                                     shist[sl(par * 64, 64),
                                           sl(prev * 512 + (h // 2) * 128, 128)],
                                     start=True, stop=True,
                                     tile_position=(par * 64, cp * 64))
                return r
            b.step("t", f_mm2, reads=[f"amt{c%2}", f"vc{tl}", "qe", f"sh{prev}",
                                      "mask8"],
                   writes=["PR", "PB2", "PB3", "PB4", "PB5"],
                   name=f"mm2_{tag}")

        def o_evac(tl):
            tag = f"{tag0}E{tl}"

            def f_oev(e, tl=tl):
                e.tensor_copy(o_sb[tl % 2][:, 0:512], PB[2][:, :])
                e.tensor_copy(o_sb[tl % 2][:, 512:1024], PB[3][:, :])
                e.drain()
                e.tensor_tensor(o_sb[tl % 2][:, 0:512],
                                o_sb[tl % 2][:, 0:512], PB[4][:, :], ALU.add)
                return e.tensor_tensor(o_sb[tl % 2][:, 512:1024],
                                       o_sb[tl % 2][:, 512:1024], PB[5][:, :],
                                       ALU.add)
            b.step("v", f_oev, reads=["PB2", "PB3", "PB4", "PB5"],
                   writes=[f"osb{tl%2}"], name=f"oev_{tag}")
            b.step("s", lambda e, tl=tl: e.activation(
                osq[:], o_sb[tl % 2][:], AF.Square),
                reads=[f"osb{tl%2}"], writes=["osq"], name=f"osq_{tag}")
            b.step("v", lambda e: e.reduce_sum(
                sums8[:], osq[:].rearrange("p (h v) -> p h v", h=8, v=128),
                AX.X), reads=["osq"], writes=["sums8"], name=f"orms_{tag}")

            def f_rms(e):
                e.activation(lnr8[:], sums8[:], AF.Ln, scale=1.0 / 128,
                             bias=EPS)
                e.drain()
                return e.activation(rmsr[:], lnr8[:], AF.Exp, scale=-0.5)
            b.step("s", f_rms, reads=["sums8", "constp"],
                   writes=["rmsr", "lnr8"], name=f"rms_{tag}")

            def f_sig(e, tl=tl):
                e.activation(osq[:], gcb[tl % 2][:], AF.Exp, scale=-1.0)
                e.drain()
                e.activation(sil2[:], osq[:], AF.Ln, bias=1.0)
                e.drain()
                return e.activation(osq[:], sil2[:], AF.Exp, scale=-1.0)
            b.step("s", f_sig, reads=[f"gc{tl%2}", "osq", "constp"],
                   writes=["osq", "sil2"], name=f"sig_{tag}")
            b.step("v", lambda e, tl=tl: e.tensor_tensor(
                sil2[:], gcb[tl % 2][:], osq[:], ALU.mult),
                reads=[f"gc{tl%2}", "osq"], writes=["sil2"], name=f"silg_{tag}")

            def f_og(e, tl=tl):
                r = None
                for h in range(8):
                    r = e.scalar_tensor_tensor(
                        ogc[tl % 3][:, ts(h, 128)],
                        o_sb[tl % 2][:, ts(h, 128)], rmsr[:, h:h + 1],
                        sil2[:, ts(h, 128)], ALU.mult, ALU.mult)
                return r
            b.step("v", f_og, reads=[f"osb{tl%2}", "rmsr", "sil2"],
                   writes=[f"ogc{tl%3}"], name=f"og_{tag}")

        for tl in range(4):
            o_pair(tl)
            o_mm1(2 * tl)
            o_mm1(2 * tl + 1)
            o_mm2(2 * tl)
            o_mm2(2 * tl + 1)
            o_evac(tl)
            if tl > 1:
                transpose_tile(ogc[(tl - 2) % 3], f"ogc{(tl-2)%3}", ogf,
                               "ogf", tl - 2, f"{tag0}T{tl-2}")
        transpose_tile(ogc[2], "ogc2", ogf, "ogf", 2, f"{tag0}T2")
        transpose_tile(ogc[0], "ogc0", ogf, "ogf", 3, f"{tag0}T3")

        def tail_pass1(tl):
            t_glob = gi * 4 + tl
            tag = f"{tag0}w{tl}"

            def f_wo(e, tl=tl):
                r = None
                for dc in range(2):
                    for k in range(8):
                        r = e.matmul(PB[dc][:, :],
                                     ogf[:, sl(k * 512 + tl * 128, 128)],
                                     wo[:, sl(k * 1024 + dc * 512, 512)],
                                     start=(k == 0), stop=(k == 7))
                return r
            b.step("t", f_wo, reads=["ogf"] + wo_deps, writes=["PB0", "PB1"],
                   name=f"wo_{tag}")
            chY.dma(lambda e, t_glob=t_glob, tl=tl: e.dma_start(
                x2t[tl % 2][:], dx[ts(t_glob, 128), :]),
                writes=[f"x2t{tl%2}"], name=f"xre_{tag}")

            def f_x2(e, tl=tl):
                e.tensor_tensor(x2t[tl % 2][:, 0:512],
                                x2t[tl % 2][:, 0:512], PB[0][:, :], ALU.add)
                return e.tensor_tensor(x2t[tl % 2][:, 512:1024],
                                       x2t[tl % 2][:, 512:1024], PB[1][:, :],
                                       ALU.add)
            b.step("v", f_x2, reads=["PB0", "PB1"],
                   writes=[f"x2t{tl%2}"], name=f"x2_{tag}")
            chS.dma(lambda e, t_glob=t_glob, tl=tl: e.dma_start(
                dx2d[ts(t_glob, 128), :], x2t[tl % 2][:]),
                reads=[f"x2t{tl%2}"], writes=[f"x2d{t_glob}"],
                name=f"x2s_{tag}")

            x2k = f"x2t{tl%2}"
            p = tl % 2
            b.step("v", lambda e, tl=tl, p=p: e.reduce_sum(
                sumx[p][:], x2t[tl % 2][:], AX.X),
                reads=[x2k], writes=[f"sumx{p}"], name=f"l2s_{tag}")
            b.step("s", lambda e, tl=tl, p=p: e.activation(
                junkS[:], x2t[tl % 2][:], AF.Square, accum_out=sumsq[p][:]),
                reads=[x2k, "constp"], writes=[f"sumsq{p}", "junkS"],
                name=f"l2q_{tag}")

            def f_stat2(e, p=p):
                e.tensor_scalar(meanv[p][:], sumx[p][:], 1.0 / D, None,
                                ALU.mult)
                e.tensor_scalar(ex2[p][:], sumsq[p][:], 1.0 / D, None,
                                ALU.mult)
                e.drain()
                return e.scalar_tensor_tensor(negvar[p][:], meanv[p][:],
                                              meanv[p][:], ex2[p][:],
                                              ALU.mult, ALU.subtract)
            b.step("v", f_stat2, reads=[f"sumx{p}", f"sumsq{p}"],
                   writes=[f"meanv{p}", f"negvar{p}", f"ex2_{p}"],
                   name=f"l2st_{tag}")

            def f_rstd2(e, p=p):
                e.activation(lnv[p][:], negvar[p][:], AF.Ln, bias=EPS,
                             scale=-1.0)
                e.drain()
                return e.activation(rstd[p][:], lnv[p][:], AF.Exp, scale=-0.5)
            b.step("s", f_rstd2, reads=[f"negvar{p}", "constp"],
                   writes=[f"rstd{p}", f"lnv{p}"], name=f"l2r_{tag}")
            b.step("v", lambda e, p=p: e.tensor_scalar(
                nbias[p][:], meanv[p][:], rstd[p][:], -1.0, ALU.mult,
                ALU.mult),
                reads=[f"meanv{p}", f"rstd{p}"], writes=[f"nbias{p}"],
                name=f"l2b_{tag}")
            b.step("s", lambda e, tl=tl, p=p: e.activation(
                xn2t[tl % 2][:], x2t[tl % 2][:], AF.Identity,
                bias=nbias[p][:], scale=rstd[p][:]),
                reads=[x2k, f"rstd{p}", f"nbias{p}"], writes=[f"xn2t{tl%2}"],
                name=f"l2a_{tag}")

        def tail_pass2(tl):
            t_glob = gi * 4 + tl
            tag = f"{tag0}w{tl}"
            transpose_tile(xn2t[tl % 2], f"xn2t{tl%2}", xn2f, "xn2f", tl, tag)

            def f_rt(e, tl=tl):
                r = None
                for k in range(8):
                    r = e.matmul(PB[5][:, 0:8],
                                 xn2f[:, sl(k * 512 + tl * 128, 128)],
                                 wr_sb[:, ts(k, 8)],
                                 start=(k == 0), stop=(k == 7))
                return r
            b.step("t", f_rt, reads=["xn2f"] + wr_deps, writes=["PB5"],
                   name=f"rt_{tag}")
            b.step("v", lambda e: e.tensor_tensor(logits[:], PB[5][:, 0:8],
                                                  brbc[:], ALU.add),
                   reads=["PB5", "brbc"], writes=["logits"], name=f"rte_{tag}")

            def f_sm1(e):
                e.reduce_max(m1[:], logits[:], AX.X)
                e.drain()
                return e.tensor_scalar(negm1[:], m1[:], -1.0, None, ALU.mult)
            b.step("v", f_sm1, reads=["logits"], writes=["m1", "negm1"],
                   name=f"sm1_{tag}")
            b.step("s", lambda e: e.activation(pex[:], logits[:], AF.Exp,
                                               bias=negm1[:], scale=1.0),
                   reads=["logits", "negm1"], writes=["pex"], name=f"sme_{tag}")

            def f_sm2(e):
                e.reduce_sum(zden[:], pex[:], AX.X)
                e.drain()
                e.reciprocal(rz[:], zden[:])
                e.drain()
                return e.tensor_scalar(probs[:], pex[:], rz[:], None, ALU.mult)
            b.step("v", f_sm2, reads=["pex"], writes=["probs", "zden", "rz"],
                   name=f"sm2_{tag}")

            def f_rank(e):
                for ei in range(8):
                    e.tensor_scalar(gtb[:, ts(ei, 8)], logits[:],
                                    logits[:, ei:ei + 1], None, ALU.is_gt)
                for ei in range(1, 8):
                    e.tensor_scalar(eqb[:, ei * 8: ei * 8 + ei],
                                    logits[:, 0:ei], logits[:, ei:ei + 1],
                                    None, ALU.is_equal)
                for ei in range(8):
                    e.reduce_sum(rk[:, ei:ei + 1], gtb[:, ts(ei, 8)], AX.X)
                for ei in range(1, 8):
                    e.reduce_sum(cnt8[:, ei:ei + 1],
                                 eqb[:, ei * 8: ei * 8 + ei], AX.X)
                e.drain()
                return e.tensor_tensor(rk[:, 1:8], rk[:, 1:8], cnt8[:, 1:8],
                                       ALU.add)
            b.step("v", f_rank, reads=["logits"],
                   writes=["rk", "gtb", "eqb", "cnt8"], name=f"rank_{tag}")

            def f_cw(e, t_glob=t_glob):
                e.tensor_scalar(keep[:], rk[:], 2.0, None, ALU.is_lt)
                e.drain()
                return e.tensor_tensor(cw_sb[:, ts(t_glob, 8)], probs[:],
                                       keep[:], ALU.mult)
            b.step("v", f_cw, reads=["rk", "probs"], writes=["cw", "keep"],
                   name=f"cw_{tag}")

        for tl in range(4):
            tail_pass1(tl)
            if tl > 0:
                tail_pass2(tl - 1)
        tail_pass2(3)

        if gi == 0:
            dbg_dump("ogf0", ogf[:], [128, 4096], BF16, ["ogf"])
            dbg_dump("xn2f0", xn2f[:], [128, 4096], BF16, ["xn2f"])
        src3 = xn2f[:].rearrange("p (k c2) -> p k c2", k=8, c2=512)
        chS.dma(lambda e, gi=gi, src3=src3: e.dma_start(
            dxn2d[:, :, ts(gi, 512)], src3),
            reads=["xn2f"], writes=[f"xn2d{gi}"], name=f"xn2s_{tag0}")

    seq = [(gi, dxprev, False, "B") for gi in range(NG)]
    seq += [(gi, dx, True, "C") for gi in range(NG)]
    for tl in range(4):
        attn_ln_pre(0, dxprev, "B", tl)
        attn_ln_tr(0, "B", tl)
    done_c = phases.startswith("BC1")

    def mk_prefetch(i):
        if i + 1 >= len(seq):
            return None
        gi2, src2, full2, ph2 = seq[i + 1]

        def pf(stage, tl, gi2=gi2, src2=src2, ph2=ph2):
            if stage == 0:
                attn_ln_pre(gi2, src2, ph2, tl)
            else:
                attn_ln_tr(gi2, ph2, tl)
        return pf

    for i, (gi, src2, full2, ph2) in enumerate(seq):
        if ph2 == "C":
            break
        attn_group(gi, src2, full2, ph2, prefetch=mk_prefetch(i))
        if phases == "B1":
            break
    b.step("v", lambda e: e.tensor_copy(shist[:, ts(8, 512)], S_sb[:]),
           reads=["S"], writes=["sh8"], name="scinit")
    if "C" not in phases:
        chS.dma(lambda e: e.dma_start(dy[0:128, 0:512], S_sb[:]),
                reads=["S"], name="bisect_out")
        fin_deps0 = [DmaTok(sid, 16 * chS.count[i])
                     for i, sid in enumerate(chS.sem_ids) if chS.count[i] > 0]
        b.step("y", lambda e: e.nop(), deps=fin_deps0, name="fin0")
        b.resolve()
        with nc.Block() as block:
            @block.tensor
            def _(eng):
                b.emit(eng, "t")

            @block.vector
            def _(eng):
                b.emit(eng, "v")

            @block.scalar
            def _(eng):
                b.emit(eng, "s")

            @block.gpsimd
            def _(eng):
                b.emit(eng, "g")

            @block.sync
            def _(eng):
                b.emit(eng, "y")
        st_a.close()
        st_f.close()
        st_p.close()
        return nc, dbg_done
    if "S0" in dbg:
        dbg_dump("S0", S_sb[:], [128, 512], F32, ["S"])
    cut = phases[3] if (phases.startswith("BC1") and len(phases) > 3) else None
    for gi in range(NG):
        attn_group(gi, dx, True, "C", cut=cut,
                   prefetch=mk_prefetch(NG + gi))
        if phases.startswith("BC1"):
            break

    dbg_dump("cw", cw_sb[:], [128, 128], F32, ["cw"])
    dbg_dump("S1", S_sb[:], [128, 512], F32, ["S"])

    if "D" not in phases:
        chS.dma(lambda e: e.dma_start(dy[0:128, 0:1024],
                                      x2t[0][:]),
                reads=["x2t0"], name="bisect_out2")
        fin_deps1 = [DmaTok(sid, 16 * chS.count[i])
                     for i, sid in enumerate(chS.sem_ids) if chS.count[i] > 0]
        b.step("y", lambda e: e.nop(), deps=fin_deps1, name="fin1")
        b.resolve()
        with nc.Block() as block:
            @block.tensor
            def _(eng):
                b.emit(eng, "t")

            @block.vector
            def _(eng):
                b.emit(eng, "v")

            @block.scalar
            def _(eng):
                b.emit(eng, "s")

            @block.gpsimd
            def _(eng):
                b.emit(eng, "g")

            @block.sync
            def _(eng):
                b.emit(eng, "y")
        st_a.close()
        st_f.close()
        st_p.close()
        return nc, dbg_done
    b.barrier("c2d")
    st_a.close()

    w1b = sb(st_f, "w1b", [128, 16384], BF16)
    w2b = sb(st_f, "w2b", [128, 16384], BF16)
    hsb = sb(st_f, "hsb", [128, 16384], BF16)
    acc = sb(st_f, "acc", [128, 8192], F32)
    xn2h = sb(st_f, "xn2h", [128, 8192], BF16)
    x2l = [sb(st_f, f"x2l{i}", [128, 1024], F32) for i in range(2)]

    for q in range(4):
        def f_cwt(e, q=q):
            r = None
            for j in range(4):
                r = e.transpose(PR[0:8, ts(j, 128)],
                                cw_sb[:, ts(q * 4 + j, 8)], identf[:])
            return r
        b.step("t", f_cwt, reads=["cw", "identf"], writes=["PR"],
               name=f"cwt{q}")
        b.step("v", lambda e, q=q: e.tensor_copy(cwf[0:8, ts(q, 512)],
                                                 PR[0:8, :]),
               reads=["PR"], writes=["cwf"], name=f"cwe{q}")

    def load_w1(Hh, e_i):
        for k in range(8):
            chG.dma(lambda eng, e_i=e_i, k=k: eng.dma_start(
                w1b[:, ts(k, 2048)], dW1[e_i, ts(k, 128), :]),
                writes=[f"w1p{k}"], name=f"w1_{Hh}e{e_i}k{k}")

    def load_w2(Hh, e_i):
        for k in range(16):
            chG.dma(lambda eng, e_i=e_i, k=k: eng.dma_start(
                w2b[:, ts(k, 1024)], dW2[e_i, ts(k, 128), :]),
                writes=[f"w2p{k}"], name=f"w2_{Hh}e{e_i}k{k}")

    for Hh in range(2):
        tagH = f"D{Hh}"
        chY.dma(lambda e, Hh=Hh: e.dma_start(
            xn2h[:].rearrange("p (k c2) -> p k c2", k=8, c2=1024),
            dxn2d[:, :, ts(Hh, 1024)]),
            reads=[f"xn2d{g}" for g in range(NG)], writes=["xn2h"],
            name=f"xnl_{tagH}")
        for tl in range(8):
            t_glob = Hh * 8 + tl

            def f_b2(e, t_glob=t_glob):
                r = None
                for dc in range(2):
                    r = e.matmul(PB[4 + dc][:, :], cwf[:, ts(t_glob, 128)],
                                 b2sb[:, ts(dc, 512)], start=True, stop=True)
                return r
            b.step("t", f_b2, reads=["cwf", "b2sb"], writes=["PB4", "PB5"],
                   name=f"b2m_{tagH}t{tl}")

            def f_ai(e, tl=tl):
                e.tensor_copy(acc[:, sl(tl * 1024, 512)], PB[4][:, :])
                return e.tensor_copy(acc[:, sl(tl * 1024 + 512, 512)],
                                     PB[5][:, :])
            b.step("v", f_ai, reads=["PB4", "PB5"], writes=[f"acc{tl}"],
                   name=f"ai_{tagH}t{tl}")
        for tl in range(8):
            t_glob = Hh * 8 + tl
            chY.dma(lambda e, t_glob=t_glob, tl=tl: e.dma_start(
                x2l[tl % 2][:], dx2d[ts(t_glob, 128), :]),
                reads=[f"x2d{t_glob}"], writes=[f"x2l{tl%2}"],
                name=f"x2l_{tagH}t{tl}")

            def f_xadd(e, tl=tl):
                e.tensor_tensor(acc[:, sl(tl * 1024, 512)],
                                acc[:, sl(tl * 1024, 512)],
                                x2l[tl % 2][:, 0:512], ALU.add)
                return e.tensor_tensor(acc[:, sl(tl * 1024 + 512, 512)],
                                       acc[:, sl(tl * 1024 + 512, 512)],
                                       x2l[tl % 2][:, 512:1024], ALU.add)
            b.step("v", f_xadd, reads=[f"x2l{tl%2}"], writes=[f"acc{tl}"],
                   name=f"xadd_{tagH}t{tl}")

        if Hh == 0:
            load_w1(Hh, 0)
        for e_i in range(NE):
            tagE = f"{tagH}e{e_i}"
            load_w2(Hh, e_i)
            bank = 0
            for m in range(16):
                for tg in range(2):
                    pbi = bank % 6
                    bank += 1

                    def f_hm(e, m=m, tg=tg, pbi=pbi):
                        r = None
                        for k in range(8):
                            r = e.matmul(PB[pbi][:, :],
                                         w1b[:, sl(k * 2048 + m * 128, 128)],
                                         xn2h[:, sl(k * 1024 + tg * 512, 512)],
                                         start=(k == 0), stop=(k == 7))
                        return r
                    b.step("t", f_hm,
                           reads=[f"w1p{k}" for k in range(8)] + ["xn2h"],
                           writes=[f"PB{pbi}"], name=f"hm_{tagE}m{m}g{tg}")
                    b.step("s", lambda e, m=m, tg=tg, e_i=e_i, pbi=pbi:
                           e.activation(
                               hsb[:, sl(m * 1024 + tg * 512, 512)],
                               PB[pbi][:, :], AF.Relu,
                               bias=b1t[:, sl(e_i * 16 + m, 1)], scale=1.0),
                           reads=[f"PB{pbi}", "b1t"], writes=[f"h{m}"],
                           name=f"he_{tagE}m{m}g{tg}")
            if e_i + 1 < NE:
                load_w1(Hh, e_i + 1)
            elif Hh == 0:
                load_w1(1, 0)
            bank = 0
            for tl in range(8):
                t_glob = Hh * 8 + tl
                for dc in range(2):
                    pbi = bank % 6
                    bank += 1

                    def f_ym(e, tl=tl, dc=dc, pbi=pbi):
                        r = None
                        for m in range(16):
                            r = e.matmul(PB[pbi][:, :],
                                         hsb[:, sl(m * 1024 + tl * 128, 128)],
                                         w2b[:, sl(m * 1024 + dc * 512, 512)],
                                         start=(m == 0), stop=(m == 15))
                        return r
                    b.step("t", f_ym,
                           reads=[f"h{m}" for m in range(16)] +
                                 [f"w2p{k}" for k in range(16)],
                           writes=[f"PB{pbi}"], name=f"ym_{tagE}t{tl}d{dc}")

                    def f_ye(e, tl=tl, dc=dc, pbi=pbi, t_glob=t_glob, e_i=e_i):
                        a = acc[:, sl(tl * 1024 + dc * 512, 512)]
                        return e.scalar_tensor_tensor(
                            a, PB[pbi][:, :],
                            cw_sb[:, sl(t_glob * 8 + e_i, 1)], a,
                            ALU.mult, ALU.add)
                    b.step("v", f_ye, reads=[f"PB{pbi}", "cw"],
                           writes=[f"acc{tl}"], name=f"ye_{tagE}t{tl}d{dc}")
        for tl in range(8):
            t_glob = Hh * 8 + tl
            chS.dma(lambda e, t_glob=t_glob, tl=tl: e.dma_start(
                dy[ts(t_glob, 128), :], acc[:, ts(tl, 1024)]),
                reads=[f"acc{tl}"], name=f"out_{tagH}t{tl}")

    fin_deps = [DmaTok(sid, 16 * chS.count[i])
                for i, sid in enumerate(chS.sem_ids) if chS.count[i] > 0]
    b.step("y", lambda e: e.nop(), deps=fin_deps, name="fin")

    b.resolve()
    with nc.Block() as block:
        @block.tensor
        def _(eng):
            b.emit(eng, "t")

        @block.vector
        def _(eng):
            b.emit(eng, "v")

        @block.scalar
        def _(eng):
            b.emit(eng, "s")

        @block.gpsimd
        def _(eng):
            b.emit(eng, "g")

        @block.sync
        def _(eng):
            b.emit(eng, "y")

    st_f.close()
    st_p.close()
    return nc, dbg_done


def make_core_inputs(full_inputs, core):
    fi = full_inputs
    bb, half = core // 2, core % 2
    x = np.ascontiguousarray(np.asarray(fi["L0_attn_x"])[bb])
    xc = np.ascontiguousarray(x[half * T:(half + 1) * T])
    xp = (np.ascontiguousarray(x[0:T]) if half == 1
          else np.zeros((T, D), dtype=np.float32))
    f32 = lambda a: np.ascontiguousarray(np.asarray(a, dtype=np.float32))
    return {
        "x": xc,
        "xprev": xp,
        "Wq": f32(fi["L0_attn_Wq"]),
        "Wk": f32(fi["L0_attn_Wk"]),
        "Wv": f32(fi["L0_attn_Wv"]),
        "Wg": f32(fi["L0_attn_Wg"]),
        "Wgk1": f32(fi["L0_attn_Wgk1"]),
        "Wgk2": f32(fi["L0_attn_Wgk2"]),
        "bgk2t": f32(fi["L0_attn_bgk2"]).reshape(4, 128).T.copy(),
        "bgk2r": f32(fi["L0_attn_bgk2"]).reshape(1, 512),
        "normw": f32(fi["L0_attn_norm_w"]).reshape(128, 1),
        "Wo": f32(fi["L0_attn_Wo"]),
        "Wr": f32(fi["L0_ffn_Wr"]),
        "br": f32(fi["L0_ffn_br"]).reshape(1, 8),
        "W1": f32(fi["L0_ffn_W1"]),
        "b1t": np.ascontiguousarray(
            f32(fi["L0_ffn_b1"]).reshape(8, 16, 128).transpose(2, 0, 1).reshape(128, 128)),
        "W2": f32(fi["L0_ffn_W2"]),
        "b2": f32(fi["L0_ffn_b2"]),
    }



_NC_CACHE = {}


def kernel(**inputs):
    from concourse.bass_utils import run_bass_kernel_spmd
    if "nc" not in _NC_CACHE:
        _NC_CACHE["nc"], _ = build_core_program(dbg=())
    nc = _NC_CACHE["nc"]
    in_maps = [make_core_inputs(inputs, c) for c in range(8)]
    res = run_bass_kernel_spmd(nc, in_maps, core_ids=list(range(8)))
    out = np.empty((4, 4096, 1024), dtype=np.float32)
    for c in range(8):
        bb, half = c // 2, c % 2
        out[bb, half * T:(half + 1) * T] = res.results[c]["y"]
    return out


# version 07 (final)
# speedup vs baseline: 1.1016x; 1.1016x over previous
import contextlib
import numpy as np
import ml_dtypes
import concourse.bass as bass
import concourse.mybir as mybir

F32 = mybir.dt.float32
BF16 = mybir.dt.bfloat16
AF = mybir.ActivationFunctionType
ALU = mybir.AluOpType
AX = mybir.AxisListType

D = 1024
T = 2048
NT = T // 128
NG = T // 512
NE = 8
LN8INV = float(np.log(0.125))
EPS = 1e-5
ENGS = "tvsgy"


class Step:
    __slots__ = ("eng", "fn", "deps", "needed", "ord", "name")

    def __init__(self, eng, fn, deps, name):
        self.eng = eng
        self.fn = fn
        self.deps = deps
        self.needed = False
        self.ord = None
        self.name = name


class DmaTok:
    __slots__ = ("sem_id", "val")

    def __init__(self, sem_id, val):
        self.sem_id = sem_id
        self.val = val


class Chan:

    def __init__(self, b, eng, nsems, tag):
        self.b = b
        self.eng = eng
        self.nsems = nsems
        self.sem_ids = [b.new_sem(f"{tag}{i}") for i in range(nsems)]
        self.count = [0] * nsems
        self.rr = 0

    def dma(self, fn, reads=(), writes=(), deps=(), name="dma"):
        i = self.rr % self.nsems
        self.rr += 1
        deps = list(deps)
        if self.count[i] > 0:
            deps.append(DmaTok(self.sem_ids[i], 16 * self.count[i]))
        self.count[i] += 1
        tok = DmaTok(self.sem_ids[i], 16 * self.count[i])
        sem_id = self.sem_ids[i]
        bb = self.b

        def wrap(eng, _fn=fn, _sem_id=sem_id):
            ins = _fn(eng)
            ins.then_inc(bb.sem_handles[_sem_id], 16)
            return None

        self.b.step(self.eng, wrap, reads=reads, writes=writes, deps=deps,
                    name=name, write_tok=tok)
        return tok


class Builder:
    def __init__(self, nc):
        self.nc = nc
        self.steps = []
        self.eng_steps = {e: [] for e in ENGS}
        self.res = {}
        self.sem_names = []
        self.sem_handles = []
        self.eng_last = {}
        self._semstack = contextlib.ExitStack()
        for e in ENGS:
            self.new_sem(f"se_{e}")

    def new_sem(self, name):
        sid = len(self.sem_names)
        self.sem_names.append(name)
        h = self._semstack.enter_context(self.nc.semaphore(name))
        self.sem_handles.append(h)
        return sid

    def step(self, eng, fn, reads=(), writes=(), deps=(), name="",
             write_tok=None):
        dl = [d for d in deps if d is not None]
        for r in reads:
            e = self.res.get(r)
            if e and e["w"] is not None:
                dl.append(e["w"])
        for w in writes:
            e = self.res.get(w)
            if e:
                if e["w"] is not None:
                    dl.append(e["w"])
                dl.extend(e["r"])
        st = Step(eng, fn, dl, name)
        self.steps.append(st)
        self.eng_steps[eng].append(st)
        tok = write_tok if write_tok is not None else st
        self.eng_last[eng] = tok
        for r in reads:
            self.res.setdefault(r, {"w": None, "r": []})["r"].append(tok)
        for w in writes:
            self.res[w] = {"w": tok, "r": []}
        return tok

    def barrier(self, name="bar"):
        lasts = {e: self.eng_last[e] for e in ENGS if e in self.eng_last}
        for e in ENGS:
            deps = [s for e2, s in lasts.items() if e2 != e]
            self.step(e, lambda eng: eng.nop(), deps=deps, name=f"{name}_{e}")

    def resolve(self):
        for st in self.steps:
            for d in st.deps:
                if isinstance(d, Step) and d.eng != st.eng:
                    d.needed = True
        for e in ENGS:
            c = 0
            for st in self.eng_steps[e]:
                if st.needed:
                    c += 1
                    st.ord = c

    def emit(self, eng_obj, e):
        hw = {}
        for st in self.eng_steps[e]:
            waits = {}
            for d in st.deps:
                if isinstance(d, DmaTok):
                    sid, val = d.sem_id, d.val
                else:
                    if d.eng == e:
                        continue
                    assert d.ord is not None, f"dep of {st.name} unmarked"
                    sid, val = ENGS.index(d.eng), d.ord
                if val > waits.get(sid, 0):
                    waits[sid] = val
            for sid, val in sorted(waits.items()):
                if val > hw.get(sid, 0):
                    eng_obj.wait_ge(self.sem_handles[sid], val)
                    hw[sid] = val
            ins = st.fn(eng_obj)
            if st.needed:
                assert ins is not None, f"step {st.name} marked, fn gave None"
                if e in ("v", "s"):
                    ins = eng_obj.drain()
                ins.then_inc(self.sem_handles[ENGS.index(e)], 1)


def ts(i, n):
    return slice(i * n, (i + 1) * n)


def sl(o, n):
    return slice(o, o + n)


def build_core_program(dbg=(), phases="BCD"):
    nc = bass.Bass("TRN2", target_bir_lowering=False)
    dbg = set(dbg)
    dbg_done = []

    dx = nc.dram_tensor("x", [T, D], F32, kind="ExternalInput").ap()
    dxprev = nc.dram_tensor("xprev", [T, D], F32, kind="ExternalInput").ap()
    dWq = nc.dram_tensor("Wq", [D, 512], F32, kind="ExternalInput").ap()
    dWk = nc.dram_tensor("Wk", [D, 512], F32, kind="ExternalInput").ap()
    dWv = nc.dram_tensor("Wv", [D, D], F32, kind="ExternalInput").ap()
    dWg = nc.dram_tensor("Wg", [D, D], F32, kind="ExternalInput").ap()
    dWgk1 = nc.dram_tensor("Wgk1", [D, 16], F32, kind="ExternalInput").ap()
    dWgk2 = nc.dram_tensor("Wgk2", [16, 512], F32, kind="ExternalInput").ap()
    dbgk2t = nc.dram_tensor("bgk2t", [128, 4], F32, kind="ExternalInput").ap()
    dbgk2r = nc.dram_tensor("bgk2r", [1, 512], F32, kind="ExternalInput").ap()
    dnormw = nc.dram_tensor("normw", [128, 1], F32, kind="ExternalInput").ap()
    dWo = nc.dram_tensor("Wo", [D, D], F32, kind="ExternalInput").ap()
    dWr = nc.dram_tensor("Wr", [D, 8], F32, kind="ExternalInput").ap()
    dbr = nc.dram_tensor("br", [1, 8], F32, kind="ExternalInput").ap()
    dW1 = nc.dram_tensor("W1", [NE, D, 2048], F32, kind="ExternalInput").ap()
    db1t = nc.dram_tensor("b1t", [128, 128], F32, kind="ExternalInput").ap()
    dW2 = nc.dram_tensor("W2", [NE, 2048, D], F32, kind="ExternalInput").ap()
    db2 = nc.dram_tensor("b2", [NE, D], F32, kind="ExternalInput").ap()
    dy = nc.dram_tensor("y", [T, D], F32, kind="ExternalOutput").ap()
    dx2d = nc.dram_tensor("x2d", [T, D], F32, kind="Internal").ap()
    dxn2d = nc.dram_tensor("xn2d", [128, 8, T], BF16, kind="Internal").ap()

    iden = np.eye(128, dtype=np.float32)
    triu = np.triu(np.ones((64, 64), dtype=np.float32))
    lbd_np = np.zeros((128, 128), dtype=np.float32)
    lbd_np[0:64, 0:64] = triu
    lbd_np[64:128, 64:128] = triu
    d_identf = nc.inline_tensor(iden, "identf_d").ap()
    d_identb = nc.inline_tensor(iden.astype(ml_dtypes.bfloat16), "identb_d").ap()
    d_mask8 = nc.inline_tensor(
        np.tile(triu, (2, 8)).astype(ml_dtypes.bfloat16), "mask8_d").ap()
    d_lbd = nc.inline_tensor(lbd_np.astype(ml_dtypes.bfloat16), "lbd_d").ap()
    d_onesf = nc.inline_tensor(np.ones((1, 128), dtype=np.float32), "onesf_d").ap()
    CONSTS = [EPS, 1.0, LN8INV, 2.0, 0.0]
    d_constp = nc.inline_tensor(
        np.tile(np.asarray(CONSTS, dtype=np.float32)[None, :], (128, 1)),
        "constp_d").ap()
    lmask_np = np.tril(np.ones((8, 8), dtype=np.float32), -1).reshape(1, 64)
    d_lmask = nc.inline_tensor(
        np.tile(lmask_np, (128, 1)), "lmask_d").ap()

    b = Builder(nc)
    st_p = contextlib.ExitStack()
    st_a = contextlib.ExitStack()
    st_f = contextlib.ExitStack()

    def sb(stk, name, shape, dt):
        return stk.enter_context(nc.sbuf_tensor(name, shape, dt))

    PB = [st_p.enter_context(nc.psum_tensor(f"pb{i}", [128, 512], F32))
          for i in range(6)]
    PT = st_p.enter_context(nc.psum_tensor("pt", [128, 1024], BF16))
    PR = st_p.enter_context(nc.psum_tensor("pr", [128, 512], F32))

    identf = sb(st_p, "identf", [128, 128], F32)
    identb = sb(st_p, "identb", [128, 128], BF16)
    mask8 = sb(st_p, "mask8", [128, 512], BF16)
    lbd = sb(st_p, "lbd", [128, 128], BF16)
    onesf = sb(st_p, "onesf", [1, 128], F32)
    constp = sb(st_p, "constp", [128, 8], F32)
    for i, v in enumerate(CONSTS):
        nc.const_aps.register(F32, float(v), constp[:, i:i + 1])
    S_sb = sb(st_p, "S_sb", [128, 512], F32)
    shist = sb(st_p, "shist", [128, 4608], BF16)
    cw_sb = sb(st_p, "cw_sb", [128, 128], F32)
    cwf = sb(st_p, "cwf", [8, 2048], BF16)
    b1t = sb(st_p, "b1t_s", [128, 128], F32)
    b2sb = sb(st_p, "b2sb", [8, 1024], BF16)
    brbc = sb(st_p, "brbc", [128, 8], F32)
    brrow = sb(st_p, "brrow", [1, 8], F32)
    wr_sb = sb(st_p, "wr", [128, 64], BF16)
    normw = sb(st_p, "normw_s", [128, 1], F32)
    bgk2t = sb(st_p, "bgk2t_s", [128, 4], F32)
    nbgk2t = sb(st_p, "nbgk2t", [128, 4], F32)
    bgk2r = sb(st_p, "bgk2r_s", [1, 512], F32)
    bgk2bc = sb(st_p, "bgk2bc", [128, 512], F32)
    sumx = [sb(st_p, f"sumx{i}", [128, 1], F32) for i in range(2)]
    sumsq = [sb(st_p, f"sumsq{i}", [128, 1], F32) for i in range(2)]
    meanv = [sb(st_p, f"meanv{i}", [128, 1], F32) for i in range(2)]
    ex2 = [sb(st_p, f"ex2_{i}", [128, 1], F32) for i in range(2)]
    negvar = [sb(st_p, f"negvar{i}", [128, 1], F32) for i in range(2)]
    lnv = [sb(st_p, f"lnv{i}", [128, 1], F32) for i in range(2)]
    rstd = [sb(st_p, f"rstd{i}", [128, 1], F32) for i in range(2)]
    nbias = [sb(st_p, f"nbias{i}", [128, 1], F32) for i in range(2)]
    m1 = sb(st_p, "m1", [128, 1], F32)
    negm1 = sb(st_p, "negm1", [128, 1], F32)
    zden = sb(st_p, "zden", [128, 1], F32)
    rz = sb(st_p, "rz", [128, 1], F32)
    logits = [sb(st_p, f"logits{i}", [128, 8], F32) for i in range(2)]
    pex = [sb(st_p, f"pex{i}", [128, 8], F32) for i in range(2)]
    probs = sb(st_p, "probs", [128, 8], F32)
    lmask = sb(st_p, "lmask", [128, 64], F32)
    rk = sb(st_p, "rk", [128, 8], F32)
    keep = sb(st_p, "keep", [128, 8], F32)
    gtb = sb(st_p, "gtb", [128, 64], F32)
    eqb = sb(st_p, "eqb", [128, 64], F32)
    cnt8 = sb(st_p, "cnt8", [128, 8], F32)

    chG = Chan(b, "g", 6, "cg")
    chY = Chan(b, "y", 4, "cy")
    chS = Chan(b, "y", 4, "cs")

    def dbg_dump(name, ap, shape, dt, reads):
        if name not in dbg:
            return
        d = nc.dram_tensor("dbg_" + name, list(shape), dt,
                           kind="ExternalOutput").ap()
        dbg_done.append(name)
        chS.dma(lambda e, _d=d, _a=ap: e.dma_start(_d, _a),
                reads=reads, name=f"dbg_{name}")

    chG.dma(lambda e: e.dma_start(identf[:], d_identf), writes=["identf"])
    chG.dma(lambda e: e.dma_start(identb[:], d_identb), writes=["identb"])
    chG.dma(lambda e: e.dma_start(mask8[:], d_mask8), writes=["mask8"])
    chG.dma(lambda e: e.dma_start(lbd[:], d_lbd), writes=["lbd"])
    chG.dma(lambda e: e.dma_start(onesf[:], d_onesf), writes=["onesf"])
    chG.dma(lambda e: e.dma_start(constp[:, 0:len(CONSTS)], d_constp),
            writes=["constp"])
    chG.dma(lambda e: e.dma_start(lmask[:], d_lmask), writes=["lmask"])
    chG.dma(lambda e: e.dma_start(bgk2t[:], dbgk2t), writes=["bgk2t"])
    chG.dma(lambda e: e.dma_start(bgk2r[:], dbgk2r), writes=["bgk2r"])
    chG.dma(lambda e: e.dma_start(normw[:], dnormw), writes=["normw"])
    chG.dma(lambda e: e.dma_start(brrow[:], dbr), writes=["brrow"])
    chG.dma(lambda e: e.dma_start(b1t[:], db1t), writes=["b1t"])
    chG.dma(lambda e: e.dma_start(b2sb[:], db2), writes=["b2sb"])

    wq = sb(st_a, "wq", [128, 4096], BF16)
    wk = sb(st_a, "wk", [128, 4096], BF16)
    wv = sb(st_a, "wv", [128, 8192], BF16)
    wg = sb(st_a, "wg", [128, 8192], BF16)
    wo = sb(st_a, "wo", [128, 8192], BF16)
    wgk1 = sb(st_a, "wgk1", [128, 128], BF16)
    wgk2 = sb(st_a, "wgk2", [16, 512], BF16)

    for k in range(8):
        chG.dma(lambda e, k=k: e.dma_start(wq[:, ts(k, 512)], dWq[ts(k, 128), :]),
                writes=[f"wq{k}"], name=f"ldwq{k}")
        chG.dma(lambda e, k=k: e.dma_start(wk[:, ts(k, 512)], dWk[ts(k, 128), :]),
                writes=[f"wk{k}"], name=f"ldwk{k}")
        chG.dma(lambda e, k=k: e.dma_start(wv[:, ts(k, 1024)], dWv[ts(k, 128), :]),
                writes=[f"wv{k}"], name=f"ldwv{k}")
        chG.dma(lambda e, k=k: e.dma_start(wg[:, ts(k, 1024)], dWg[ts(k, 128), :]),
                writes=[f"wg{k}"], name=f"ldwg{k}")
        chG.dma(lambda e, k=k: e.dma_start(wo[:, ts(k, 1024)], dWo[ts(k, 128), :]),
                writes=[f"wo{k}"], name=f"ldwo{k}")
        chG.dma(lambda e, k=k: e.dma_start(wgk1[:, ts(k, 16)], dWgk1[ts(k, 128), :]),
                writes=[f"wgk1_{k}"], name=f"ldwgk1{k}")
        chG.dma(lambda e, k=k: e.dma_start(wr_sb[:, ts(k, 8)], dWr[ts(k, 128), :]),
                writes=[f"wr{k}"], name=f"ldwr{k}")
    chG.dma(lambda e: e.dma_start(wgk2[:], dWgk2), writes=["wgk2"])

    b.step("v", lambda e: e.tensor_scalar(
        nbgk2t[:], bgk2t[:], -1.0, None, ALU.mult),
        reads=["bgk2t"], writes=["nbgk2t"], name="nbgk2t")
    b.step("t", lambda e: e.matmul(PR[:, 0:8], onesf[0:1, :], brrow[:],
                                   start=True, stop=True),
           reads=["onesf", "brrow"], writes=["PR"], name="brmm")
    b.step("v", lambda e: e.tensor_copy(brbc[:], PR[:, 0:8]),
           reads=["PR"], writes=["brbc"], name="brbcv")
    b.step("t", lambda e: e.matmul(PR[:, :], onesf[0:1, :], bgk2r[:],
                                   start=True, stop=True),
           reads=["onesf", "bgk2r"], writes=["PR"], name="bgmm")
    b.step("v", lambda e: e.tensor_copy(bgk2bc[:], PR[:, :]),
           reads=["PR"], writes=["bgk2bc"], name="bgbcv")
    b.step("v", lambda e: e.memset(S_sb[:], 0.0), writes=["S"], name="s0")
    for k in range(8):
        b.step("v", lambda e, k=k: e.tensor_scalar(
            wo[:, ts(k, 1024)], wo[:, ts(k, 1024)], normw[:], None, ALU.mult),
            reads=[f"wo{k}", "normw"], writes=[f"wo{k}"], name=f"wosc{k}")

    xin = [sb(st_a, f"xin{i}", [128, 1024], F32) for i in range(2)]
    junkS = sb(st_a, "junkS", [128, 1024], BF16)
    xnt = [sb(st_a, f"xnt{i}", [128, 1024], BF16) for i in range(3)]
    xnfb = [sb(st_a, f"xnf{i}", [128, 4096], BF16) for i in range(2)]
    z1f = sb(st_a, "z1f", [16, 512], BF16)
    uf1 = [sb(st_a, f"uf1_{i}", [128, 512], BF16) for i in range(2)]
    spf = sb(st_a, "spf", [128, 2048], F32)
    facq = [sb(st_a, "facq0", [128, 512], F32)] * 2
    qe = sb(st_a, "qe", [128, 2048], BF16)
    ke = sb(st_a, "ke", [128, 2048], BF16)
    vcg = sb(st_a, "vcg", [128, 4096], BF16)
    gcb = [sb(st_a, f"gc{i}", [128, 1024], BF16) for i in range(2)]
    ktb = [sb(st_a, f"ktb{i}", [128, 512], BF16) for i in range(2)]
    utb = [sb(st_a, f"utb{i}", [128, 512], BF16) for i in range(3)]
    ketb = [sb(st_a, f"ket{i}", [128, 512], BF16) for i in range(2)]
    zb = sb(st_a, "zb", [128, 512], BF16)
    kef = sb(st_a, "kef", [128, 512], BF16)
    decb = [sb(st_a, f"dec{i}", [128, 4], F32) for i in range(2)]
    amtb = [sb(st_a, f"amt{i}", [128, 512], BF16) for i in range(4)]
    o_sb = [sb(st_a, f"o_sb{i}", [128, 1024], BF16) for i in range(2)]
    osq = sb(st_a, "osq", [128, 1024], BF16)
    sums8 = sb(st_a, "sums8", [128, 8], F32)
    lnr8 = sb(st_a, "lnr8", [128, 8], F32)
    rmsr = sb(st_a, "rmsr", [128, 8], F32)
    ogc = [sb(st_a, f"ogc{i}", [128, 1024], BF16) for i in range(3)]
    ogf = sb(st_a, "ogf", [128, 4096], BF16)
    x2t = [sb(st_a, f"x2t{i}", [128, 1024], F32) for i in range(2)]
    xn2t = [sb(st_a, "xn2t0", [128, 1024], BF16)] * 2
    xn2f = sb(st_a, "xn2f", [128, 4096], BF16)

    wq_deps = [f"wq{k}" for k in range(8)]
    wk_deps = [f"wk{k}" for k in range(8)]
    wv_deps = [f"wv{k}" for k in range(8)]
    wg_deps = [f"wg{k}" for k in range(8)]
    wo_deps = [f"wo{k}" for k in range(8)]
    wgk1_deps = [f"wgk1_{k}" for k in range(8)]
    wr_deps = [f"wr{k}" for k in range(8)]

    def ln_tile(src_dram, t_glob, buf_i, out_xnt, out_key, tag):
        xk = f"xin{buf_i}"
        p = buf_i % 2
        chY.dma(lambda e: e.dma_start(xin[buf_i][:], src_dram[ts(t_glob, 128), :]),
                writes=[xk], name=f"ldx_{tag}")
        b.step("v", lambda e: e.reduce_sum(sumx[p][:], xin[buf_i][:], AX.X),
               reads=[xk], writes=[f"sumx{p}"], name=f"lns_{tag}")
        b.step("s", lambda e: e.activation(junkS[:], xin[buf_i][:], AF.Square,
                                           accum_out=sumsq[p][:]),
               reads=[xk, "constp"], writes=[f"sumsq{p}", "junkS"],
               name=f"lnq_{tag}")

        def f_stat(e):
            e.tensor_scalar(meanv[p][:], sumx[p][:], 1.0 / D, None, ALU.mult)
            e.tensor_scalar(ex2[p][:], sumsq[p][:], 1.0 / D, None, ALU.mult)
            e.drain()
            return e.scalar_tensor_tensor(negvar[p][:], meanv[p][:],
                                          meanv[p][:], ex2[p][:],
                                          ALU.mult, ALU.subtract)
        b.step("v", f_stat, reads=[f"sumx{p}", f"sumsq{p}"],
               writes=[f"meanv{p}", f"negvar{p}", f"ex2_{p}"],
               name=f"lnst_{tag}")

        def f_rstd(e):
            e.activation(lnv[p][:], negvar[p][:], AF.Ln, bias=EPS, scale=-1.0)
            e.drain()
            return e.activation(rstd[p][:], lnv[p][:], AF.Exp, scale=-0.5)
        b.step("s", f_rstd, reads=[f"negvar{p}", "constp"],
               writes=[f"rstd{p}", f"lnv{p}"], name=f"lnr_{tag}")
        b.step("v", lambda e: e.tensor_scalar(nbias[p][:], meanv[p][:],
                                              rstd[p][:], -1.0,
                                              ALU.mult, ALU.mult),
               reads=[f"meanv{p}", f"rstd{p}"], writes=[f"nbias{p}"],
               name=f"lnb_{tag}")
        b.step("s", lambda e: e.activation(out_xnt[:], xin[buf_i][:],
                                           AF.Identity, bias=nbias[p][:],
                                           scale=rstd[p][:]),
               reads=[xk, f"rstd{p}", f"nbias{p}"], writes=[out_key],
               name=f"lna_{tag}")

    def transpose_tile(src, src_key, dst, dst_key, tl, tag):
        def f_tr(e):
            r = None
            for k in range(8):
                r = e.transpose(PT[:, ts(k, 128)], src[:, ts(k, 128)],
                                identb[:])
            return r
        b.step("t", f_tr, reads=[src_key, "identb"], writes=["PT"],
               name=f"tr_{tag}")
        dst3 = dst[:].rearrange("p (k c) -> p k c", k=8, c=512)[:, :, ts(tl, 128)]
        src3 = PT[:].rearrange("p (k c) -> p k c", k=8, c=128)
        b.step("v", lambda e: e.tensor_copy(dst3, src3),
               reads=["PT"], writes=[dst_key], name=f"tre_{tag}")

    def attn_ln_pre(gi, src_dram, ph, tl):
        t_glob = gi * 4 + tl
        tag = f"{ph}{gi}t{tl}"
        ln_tile(src_dram, t_glob, tl % 2, xnt[tl % 3], f"xnt{tl%3}", tag)

    def attn_ln_tr(gi, ph, tl):
        tag = f"{ph}{gi}t{tl}"
        transpose_tile(xnt[tl % 3], f"xnt{tl%3}", xnfb[gi % 2], f"xnf{gi%2}",
                       tl, tag)

    def attn_group(gi, src_dram, full, ph, cut=None, prefetch=None):
        tag0 = f"{ph}{gi}"
        xnf = xnfb[gi % 2]
        xnfk = f"xnf{gi%2}"

        def f_z1(e):
            r = None
            for k in range(8):
                r = e.matmul(PR[0:16, :], wgk1[:, ts(k, 16)], xnf[:, ts(k, 512)],
                             start=(k == 0), stop=(k == 7))
            return r
        b.step("t", f_z1, reads=[xnfk] + wgk1_deps, writes=["PR"],
               name=f"z1_{tag0}")
        b.step("v", lambda e: e.tensor_copy(z1f[:], PR[0:16, :]),
               reads=["PR"], writes=["z1f"], name=f"z1e_{tag0}")

        if full:
            for pc in range(4):
                b.step("t", lambda e, pc=pc: e.matmul(
                    PB[pc % 2][:, :], wgk2[:, ts(pc, 128)], z1f[:],
                    start=True, stop=True),
                    reads=["z1f", "wgk2"], writes=[f"PB{pc%2}"],
                    name=f"zf_{tag0}p{pc}")

                def f_uf(e, pc=pc):
                    e.activation(zb[:], PB[pc % 2][:, :], AF.Exp,
                                 bias=nbgk2t[:, pc:pc + 1], scale=-1.0)
                    e.drain()
                    return e.activation(uf1[pc % 2][:], zb[:], AF.Ln, bias=1.0)
                b.step("s", f_uf, reads=[f"PB{pc%2}", "nbgk2t", "constp"],
                       writes=[f"uf{pc%2}", "zbuf"], name=f"uf_{tag0}p{pc}")

                def f_spf(e, pc=pc):
                    r = None
                    for c8 in range(8):
                        r = e.tensor_tensor_scan(
                            spf[:, sl(pc * 512 + c8 * 64, 64)],
                            uf1[pc % 2][:, ts(c8, 64)],
                            uf1[pc % 2][:, ts(c8, 64)],
                            0.0, ALU.add, ALU.bypass)
                    return r
                b.step("v", f_spf, reads=[f"uf{pc%2}", "constp"], writes=["spf"],
                       name=f"spf_{tag0}p{pc}")
            for pc in range(4):
                def f_qp(e, pc=pc):
                    r = None
                    for k in range(8):
                        r = e.matmul(PB[2 + pc % 2][:, :],
                                     wq[:, sl(k * 512 + pc * 128, 128)],
                                     xnf[:, ts(k, 512)],
                                     start=(k == 0), stop=(k == 7))
                    return r
                b.step("t", f_qp, reads=[xnfk] + wq_deps,
                       writes=[f"PB{2+pc%2}"], name=f"qp_{tag0}p{pc}")
                b.step("s", lambda e, pc=pc: e.activation(
                    facq[pc % 2][:], spf[:, ts(pc, 512)], AF.Exp,
                    scale=-1.0 / 16, bias=LN8INV),
                    reads=["spf", "constp"], writes=["facq0"],
                    name=f"qf_{tag0}p{pc}")
                b.step("v", lambda e, pc=pc: e.tensor_tensor(
                    qe[:, ts(pc, 512)], PB[2 + pc % 2][:, :],
                    facq[pc % 2][:], ALU.mult),
                    reads=[f"PB{2+pc%2}", "facq0"], writes=["qe"],
                    name=f"qe_{tag0}p{pc}")
            for pc in range(4):
                def f_kp(e, pc=pc):
                    r = None
                    for k in range(8):
                        r = e.matmul(PB[4 + pc % 2][:, :],
                                     wk[:, sl(k * 512 + pc * 128, 128)],
                                     xnf[:, ts(k, 512)],
                                     start=(k == 0), stop=(k == 7))
                    return r
                b.step("t", f_kp, reads=[xnfk] + wk_deps,
                       writes=[f"PB{4+pc%2}"], name=f"kp_{tag0}p{pc}")
                b.step("s", lambda e, pc=pc: e.activation(
                    facq[pc % 2][:], spf[:, ts(pc, 512)], AF.Exp,
                    scale=1.0 / 16),
                    reads=["spf", "facq0"], writes=["facq0"],
                    name=f"kf_{tag0}p{pc}")
                b.step("v", lambda e, pc=pc: e.tensor_tensor(
                    ke[:, ts(pc, 512)], PB[4 + pc % 2][:, :],
                    facq[pc % 2][:], ALU.mult),
                    reads=[f"PB{4+pc%2}", "facq0"], writes=["ke"],
                    name=f"ke_{tag0}p{pc}")

        if cut == "f":
            return
        def pair_ops(tl):
            tag = f"{tag0}P{tl}"
            tok0 = tl * 128

            def f_vp(e, tok0=tok0):
                r = None
                for dc in range(2):
                    for k in range(8):
                        r = e.matmul(PB[dc][:, :],
                                     xnf[:, sl(k * 512 + tok0, 128)],
                                     wv[:, sl(k * 1024 + dc * 512, 512)],
                                     start=(k == 0), stop=(k == 7))
                return r
            b.step("t", f_vp, reads=[xnfk] + wv_deps, writes=["PB0", "PB1"],
                   name=f"vp_{tag}")

            def f_ve(e, tl=tl):
                e.activation(vcg[:, sl(tl * 1024, 512)], PB[0][:, :], AF.Copy)
                return e.activation(vcg[:, sl(tl * 1024 + 512, 512)],
                                    PB[1][:, :], AF.Copy)
            b.step("s", f_ve, reads=["PB0", "PB1"], writes=[f"vc{tl}"],
                   name=f"ve_{tag}")

            def f_kt(e, tok0=tok0):
                r = None
                for k in range(8):
                    r = e.matmul(PB[2][:, :], xnf[:, sl(k * 512 + tok0, 128)],
                                 wk[:, ts(k, 512)],
                                 start=(k == 0), stop=(k == 7))
                return r
            b.step("t", f_kt, reads=[xnfk] + wk_deps, writes=["PB2"],
                   name=f"kt_{tag}")
            b.step("s", lambda e, tl=tl: e.activation(ktb[tl % 2][:],
                                                      PB[2][:, :], AF.Copy),
                   reads=["PB2"], writes=[f"ktb{tl%2}"], name=f"kte_{tag}")

            b.step("t", lambda e, tok0=tok0: e.matmul(
                PB[5][:, :], z1f[0:16, sl(tok0, 128)], wgk2[:],
                start=True, stop=True),
                reads=["z1f", "wgk2"], writes=["PB5"], name=f"zt_{tag}")
            b.step("v", lambda e: e.tensor_tensor(zb[:], PB[5][:, :],
                                                  bgk2bc[:], ALU.add),
                   reads=["PB5", "bgk2bc"], writes=["zbuf"], name=f"ub_{tag}")

            def f_ut(e, tl=tl):
                e.activation(zb[:], zb[:], AF.Exp, scale=-1.0)
                e.drain()
                return e.activation(utb[tl % 3][:], zb[:], AF.Ln, bias=1.0)
            b.step("s", f_ut, reads=["zbuf", "constp"],
                   writes=[f"ut{tl%3}", "zbuf"], name=f"ut_{tag}")

        def pair_ops2(tl):
            tag = f"{tag0}Q{tl}"
            b.step("t", lambda e, tl=tl: e.matmul(
                PB[3][:, :], lbd[:], utb[tl % 3][:], start=True, stop=True),
                reads=[f"ut{tl%3}", "lbd"], writes=["PB3"], name=f"spt_{tag}")
            b.step("s", lambda e: e.activation(kef[:], PB[3][:, :], AF.Exp,
                                               scale=1.0 / 16),
                   reads=["PB3"], writes=["kef"], name=f"kef_{tag}")
            b.step("v", lambda e, tl=tl: e.tensor_tensor(
                ketb[tl % 2][:], ktb[tl % 2][:], kef[:], ALU.mult),
                reads=[f"ktb{tl%2}", "kef"], writes=[f"ket{tl%2}"],
                name=f"ket_{tag}")

        def chunk_ops(c):
            tag = f"{tag0}c{c}"
            tl = c // 2
            cp = c % 2
            slot = (gi * 8 + c) % 9
            if full:
                spf3 = spf[:].rearrange("p (h c2) -> p h c2", h=4, c2=512)
                b.step("s", lambda e, c=c, spf3=spf3: e.activation(
                    decb[c % 2][:], spf3[:, :, sl(c * 64 + 63, 1)], AF.Exp,
                    scale=-1.0 / 16),
                    reads=["spf"], writes=[f"dec{c%2}"], name=f"dec_{tag}")
            else:
                def f_decmm(e, tl=tl, cp=cp):
                    r = None
                    for h in range(8):
                        r = e.matmul(PR[sl((h % 2) * 64, 64), sl(h // 2, 1)],
                                     utb[tl % 3][sl(cp * 64, 64), ts(h, 64)],
                                     mask8[sl(cp * 64, 64), 63:64],
                                     start=True, stop=True,
                                     tile_position=(cp * 64, (h % 2) * 64))
                    return r
                b.step("t", f_decmm, reads=[f"ut{tl%3}", "mask8"], writes=["PR"],
                       name=f"decm_{tag}")
                b.step("s", lambda e, c=c: e.activation(
                    decb[c % 2][:], PR[:, 0:4], AF.Exp, scale=-1.0 / 16),
                    reads=["PR"], writes=[f"dec{c%2}"], name=f"dec_{tag}")

            m3b = 4 if cp == 0 else 3

            def f_mm3(e, tl=tl, cp=cp, m3b=m3b):
                r = None
                for h in range(8):
                    r = e.matmul(PB[m3b][sl((h % 2) * 64, 64), ts(h // 2, 128)],
                                 ketb[tl % 2][sl(cp * 64, 64), ts(h, 64)],
                                 vcg[sl(cp * 64, 64), sl(tl * 1024 + h * 128, 128)],
                                 start=True, stop=True,
                                 tile_position=(cp * 64, (h % 2) * 64))
                return r
            b.step("t", f_mm3, reads=[f"ket{tl%2}", f"vc{tl}"],
                   writes=[f"PB{m3b}"], name=f"mm3_{tag}")

            def f_supd(e, c=c, slot=slot, m3b=m3b):
                e.tensor_tensor(S_sb[:], S_sb[:], PB[m3b][:, :], ALU.add)
                e.drain()
                S3 = S_sb[:].rearrange("p (c2 j) -> p c2 j", c2=4, j=128)
                d3 = decb[c % 2][:].rearrange("p (c2 j) -> p c2 j", c2=4, j=1)
                S3b, d3b = bass.broadcast_tensor_aps(S3, d3)
                e.tensor_tensor(S3, S3b, d3b, ALU.mult)
                if full:
                    e.drain()
                    return e.tensor_copy(shist[:, ts(slot, 512)], S_sb[:])
                return e.engine_nop()
            b.step("v", f_supd, reads=[f"PB{m3b}", f"dec{c%2}", "S"],
                   writes=(["S", f"sh{slot}"] if full else ["S"]),
                   name=f"supd_{tag}")

        for tl in range(4):
            pair_ops(tl)
            if prefetch is not None:
                prefetch(0, tl)
            if tl >= 1:
                pair_ops2(tl - 1)
            if tl >= 2:
                chunk_ops(2 * (tl - 2))
                chunk_ops(2 * (tl - 2) + 1)
                if prefetch is not None:
                    prefetch(1, tl - 2)
        pair_ops2(3)
        chunk_ops(4)
        chunk_ops(5)
        if prefetch is not None:
            prefetch(1, 2)
        chunk_ops(6)
        chunk_ops(7)
        if prefetch is not None:
            prefetch(1, 3)

        if not full:
            return
        if cut == "s":
            return

        def o_pair(tl):
            tag = f"{tag0}G{tl}"
            tok0 = tl * 128

            def f_gp(e, tok0=tok0):
                r = None
                for dc in range(2):
                    for k in range(8):
                        r = e.matmul(PB[dc][:, :],
                                     xnf[:, sl(k * 512 + tok0, 128)],
                                     wg[:, sl(k * 1024 + dc * 512, 512)],
                                     start=(k == 0), stop=(k == 7))
                return r
            b.step("t", f_gp, reads=[xnfk] + wg_deps, writes=["PB0", "PB1"],
                   name=f"gp_{tag}")

            def f_ge(e, tl=tl):
                e.activation(gcb[tl % 2][:, 0:512], PB[0][:, :], AF.Copy)
                return e.activation(gcb[tl % 2][:, 512:1024], PB[1][:, :],
                                    AF.Copy)
            b.step("s", f_ge, reads=["PB0", "PB1"], writes=[f"gc{tl%2}"],
                   name=f"ge_{tag}")

            def f_sig(e, tl=tl):
                e.activation(osq[:], gcb[tl % 2][:], AF.Exp, scale=-1.0)
                e.drain()
                e.activation(osq[:], osq[:], AF.Ln, bias=1.0)
                e.drain()
                return e.activation(osq[:], osq[:], AF.Exp, scale=-1.0)
            b.step("s", f_sig, reads=[f"gc{tl%2}", "constp"],
                   writes=["osq"], name=f"sig_{tag}")
            b.step("v", lambda e, tl=tl: e.tensor_tensor(
                gcb[tl % 2][:], gcb[tl % 2][:], osq[:], ALU.mult),
                reads=["osq"], writes=[f"gc{tl%2}"], name=f"silg_{tag}")

        def o_mm1(c):
            tag = f"{tag0}o{c}"
            cp = c % 2

            def f_mm1(e, c=c, cp=cp):
                r = None
                for par in range(2):
                    if par == 1:
                        col = (1 - cp) * 64
                        e.matmul(PB[4][sl(col, 4), 0:4],
                                 mask8[0:64, 0:4], mask8[0:64, 0:4],
                                 start=True, stop=True,
                                 tile_position=(0, col))
                    for h in range(par, 8, 2):
                        r = e.matmul(PR[sl(cp * 64, 64), ts(h, 64)],
                                     ke[sl(par * 64, 64),
                                        sl((h // 2) * 512 + c * 64, 64)],
                                     qe[sl(par * 64, 64),
                                        sl((h // 2) * 512 + c * 64, 64)],
                                     start=True, stop=True,
                                     tile_position=(par * 64, cp * 64))
                return r
            b.step("t", f_mm1, reads=["ke", "qe", "mask8"],
                   writes=["PR", "PB4"], name=f"mm1_{tag}")
            b.step("v", lambda e, c=c, cp=cp: e.tensor_tensor(
                amtb[c % 4][sl(cp * 64, 64), :], PR[sl(cp * 64, 64), :],
                mask8[sl(cp * 64, 64), :], ALU.mult),
                reads=["PR", "mask8"], writes=[f"amt{c%4}"], name=f"am_{tag}")

        def o_mm2(c):
            tag = f"{tag0}m{c}"
            tl = c // 2
            cp = c % 2
            prev = (gi * 8 + c - 1) % 9

            def dummy_mm(e, row, col):
                e.matmul(PB[0][sl(col, 4), 0:4], mask8[sl(row, 64), 0:4],
                         mask8[sl(row, 64), 0:4], start=True, stop=True,
                         tile_position=(row, col))

            def f_mm2(e, c=c, tl=tl, cp=cp, prev=prev):
                r = None
                for h in range(8):
                    r = e.matmul(PB[2 + h // 4][sl(cp * 64, 64), ts(h % 4, 128)],
                                 amtb[c % 4][sl(cp * 64, 64), ts(h, 64)],
                                 vcg[sl(cp * 64, 64),
                                     sl(tl * 1024 + h * 128, 128)],
                                 start=True, stop=True,
                                 tile_position=(cp * 64, cp * 64))
                for par in range(2):
                    if (par == 0 and cp == 1) or par == 1:
                        dummy_mm(e, (1 - par) * 64 if par == 1 else 64,
                                 (1 - cp) * 64)
                    for h in range(par, 8, 2):
                        r = e.matmul(PB[4 + h // 4][sl(cp * 64, 64),
                                                    ts(h % 4, 128)],
                                     qe[sl(par * 64, 64),
                                        sl((h // 2) * 512 + c * 64, 64)],
                                     shist[sl(par * 64, 64),
                                           sl(prev * 512 + (h // 2) * 128, 128)],
                                     start=True, stop=True,
                                     tile_position=(par * 64, cp * 64))
                return r
            b.step("t", f_mm2, reads=[f"amt{c%4}", f"vc{tl}", "qe", f"sh{prev}",
                                      "mask8"],
                   writes=["PB0", "PB2", "PB3", "PB4", "PB5"],
                   name=f"mm2_{tag}")

        def o_evac(tl):
            tag = f"{tag0}E{tl}"

            def f_oevc(e, tl=tl):
                e.activation(o_sb[tl % 2][:, 0:512], PB[2][:, :], AF.Copy)
                return e.activation(o_sb[tl % 2][:, 512:1024], PB[3][:, :],
                                    AF.Copy)
            b.step("s", f_oevc, reads=["PB2", "PB3"], writes=[f"osb{tl%2}"],
                   name=f"oevc_{tag}")

            def f_oev(e, tl=tl):
                e.tensor_tensor(o_sb[tl % 2][:, 0:512],
                                o_sb[tl % 2][:, 0:512], PB[4][:, :], ALU.add)
                return e.tensor_tensor(o_sb[tl % 2][:, 512:1024],
                                       o_sb[tl % 2][:, 512:1024], PB[5][:, :],
                                       ALU.add)
            b.step("v", f_oev, reads=["PB4", "PB5", f"osb{tl%2}"],
                   writes=[f"osb{tl%2}"], name=f"oev_{tag}")
            b.step("s", lambda e, tl=tl: e.activation(
                osq[:], o_sb[tl % 2][:], AF.Square),
                reads=[f"osb{tl%2}"], writes=["osq"], name=f"osq_{tag}")
            b.step("v", lambda e: e.reduce_sum(
                sums8[:], osq[:].rearrange("p (h v) -> p h v", h=8, v=128),
                AX.X), reads=["osq"], writes=["sums8"], name=f"orms_{tag}")

            def f_rms(e):
                e.activation(lnr8[:], sums8[:], AF.Ln, scale=1.0 / 128,
                             bias=EPS)
                e.drain()
                return e.activation(rmsr[:], lnr8[:], AF.Exp, scale=-0.5)
            b.step("s", f_rms, reads=["sums8", "constp"],
                   writes=["rmsr", "lnr8"], name=f"rms_{tag}")

            def f_og(e, tl=tl):
                o3 = o_sb[tl % 2][:].rearrange("p (h v) -> p h v", h=8, v=128)
                r3 = rmsr[:].rearrange("p (h v) -> p h v", h=8, v=1)
                o3b, r3b = bass.broadcast_tensor_aps(o3, r3)
                e.tensor_tensor(o3, o3b, r3b, ALU.mult)
                e.drain()
                return e.tensor_tensor(ogc[tl % 3][:], o_sb[tl % 2][:],
                                       gcb[tl % 2][:], ALU.mult)
            b.step("v", f_og, reads=[f"osb{tl%2}", "rmsr", f"gc{tl%2}"],
                   writes=[f"ogc{tl%3}", f"osb{tl%2}"], name=f"og_{tag}")

        o_mm1(0)
        o_mm1(1)
        for tl in range(4):
            o_pair(tl)
            if tl < 3:
                o_mm1(2 * tl + 2)
                o_mm1(2 * tl + 3)
            o_mm2(2 * tl)
            o_mm2(2 * tl + 1)
            o_evac(tl)
            if tl > 1:
                transpose_tile(ogc[(tl - 2) % 3], f"ogc{(tl-2)%3}", ogf,
                               f"ogf{tl-2}", tl - 2, f"{tag0}T{tl-2}")

        def tail_pass1(tl):
            t_glob = gi * 4 + tl
            tag = f"{tag0}w{tl}"

            def f_wo(e, tl=tl):
                r = None
                for dc in range(2):
                    for k in range(8):
                        r = e.matmul(PB[dc][:, :],
                                     ogf[:, sl(k * 512 + tl * 128, 128)],
                                     wo[:, sl(k * 1024 + dc * 512, 512)],
                                     start=(k == 0), stop=(k == 7))
                return r
            b.step("t", f_wo, reads=[f"ogf{tl}"] + wo_deps,
                   writes=["PB0", "PB1"], name=f"wo_{tag}")
            chY.dma(lambda e, t_glob=t_glob, tl=tl: e.dma_start(
                x2t[tl % 2][:], dx[ts(t_glob, 128), :]),
                writes=[f"x2t{tl%2}"], name=f"xre_{tag}")

            def f_x2(e, tl=tl):
                e.tensor_tensor(x2t[tl % 2][:, 0:512],
                                x2t[tl % 2][:, 0:512], PB[0][:, :], ALU.add)
                return e.tensor_tensor(x2t[tl % 2][:, 512:1024],
                                       x2t[tl % 2][:, 512:1024], PB[1][:, :],
                                       ALU.add)
            b.step("v", f_x2, reads=["PB0", "PB1"],
                   writes=[f"x2t{tl%2}"], name=f"x2_{tag}")
            chS.dma(lambda e, t_glob=t_glob, tl=tl: e.dma_start(
                dx2d[ts(t_glob, 128), :], x2t[tl % 2][:]),
                reads=[f"x2t{tl%2}"], writes=[f"x2d{t_glob}"],
                name=f"x2s_{tag}")

            x2k = f"x2t{tl%2}"
            p = tl % 2
            b.step("v", lambda e, tl=tl, p=p: e.reduce_sum(
                sumx[p][:], x2t[tl % 2][:], AX.X),
                reads=[x2k], writes=[f"sumx{p}"], name=f"l2s_{tag}")
            b.step("s", lambda e, tl=tl, p=p: e.activation(
                junkS[:], x2t[tl % 2][:], AF.Square, accum_out=sumsq[p][:]),
                reads=[x2k, "constp"], writes=[f"sumsq{p}", "junkS"],
                name=f"l2q_{tag}")

            def f_stat2(e, p=p):
                e.tensor_scalar(meanv[p][:], sumx[p][:], 1.0 / D, None,
                                ALU.mult)
                e.tensor_scalar(ex2[p][:], sumsq[p][:], 1.0 / D, None,
                                ALU.mult)
                e.drain()
                return e.scalar_tensor_tensor(negvar[p][:], meanv[p][:],
                                              meanv[p][:], ex2[p][:],
                                              ALU.mult, ALU.subtract)
            b.step("v", f_stat2, reads=[f"sumx{p}", f"sumsq{p}"],
                   writes=[f"meanv{p}", f"negvar{p}", f"ex2_{p}"],
                   name=f"l2st_{tag}")

            def f_rstd2(e, p=p):
                e.activation(lnv[p][:], negvar[p][:], AF.Ln, bias=EPS,
                             scale=-1.0)
                e.drain()
                return e.activation(rstd[p][:], lnv[p][:], AF.Exp, scale=-0.5)
            b.step("s", f_rstd2, reads=[f"negvar{p}", "constp"],
                   writes=[f"rstd{p}", f"lnv{p}"], name=f"l2r_{tag}")
            b.step("v", lambda e, p=p: e.tensor_scalar(
                nbias[p][:], meanv[p][:], rstd[p][:], -1.0, ALU.mult,
                ALU.mult),
                reads=[f"meanv{p}", f"rstd{p}"], writes=[f"nbias{p}"],
                name=f"l2b_{tag}")
            b.step("s", lambda e, tl=tl, p=p: e.activation(
                xn2t[tl % 2][:], x2t[tl % 2][:], AF.Identity,
                bias=nbias[p][:], scale=rstd[p][:]),
                reads=[x2k, f"rstd{p}", f"nbias{p}"], writes=[f"xn2t{tl%2}"],
                name=f"l2a_{tag}")

        def tail_tr(tl):
            tag = f"{tag0}w{tl}"
            transpose_tile(xn2t[tl % 2], f"xn2t{tl%2}", xn2f, f"xn2f_{tl}",
                           tl, tag)

        def tail_rt(tl):
            tag = f"{tag0}w{tl}"

            def f_rt(e, tl=tl):
                r = None
                for k in range(8):
                    r = e.matmul(PR[:, 0:8],
                                 xn2f[:, sl(k * 512 + tl * 128, 128)],
                                 wr_sb[:, ts(k, 8)],
                                 start=(k == 0), stop=(k == 7))
                return r
            b.step("t", f_rt, reads=[f"xn2f_{tl}"] + wr_deps, writes=["PR"],
                   name=f"rt_{tag}")
            b.step("v", lambda e, tl=tl: e.tensor_tensor(
                logits[tl % 2][:], PR[:, 0:8], brbc[:], ALU.add),
                reads=["PR", "brbc"], writes=[f"logits{tl%2}"],
                name=f"rte_{tag}")

        def tail_sm(tl):
            t_glob = gi * 4 + tl
            tag = f"{tag0}w{tl}"
            lg = logits[tl % 2]
            lgk = f"logits{tl%2}"

            def f_sm1(e, lg=lg):
                e.reduce_max(m1[:], lg[:], AX.X)
                e.drain()
                return e.tensor_scalar(negm1[:], m1[:], -1.0, None, ALU.mult)
            b.step("v", f_sm1, reads=[lgk], writes=["m1", "negm1"],
                   name=f"sm1_{tag}")
            b.step("s", lambda e, lg=lg, tl=tl: e.activation(
                pex[tl % 2][:], lg[:], AF.Exp, bias=negm1[:], scale=1.0),
                reads=[lgk, "negm1"], writes=[f"pex{tl%2}"],
                name=f"sme_{tag}")

            def f_sm2(e, tl=tl):
                e.reduce_sum(zden[:], pex[tl % 2][:], AX.X)
                e.drain()
                e.reciprocal(rz[:], zden[:])
                e.drain()
                return e.tensor_scalar(probs[:], pex[tl % 2][:], rz[:], None,
                                       ALU.mult)
            b.step("v", f_sm2, reads=[f"pex{tl%2}"],
                   writes=["probs", "zden", "rz"], name=f"sm2_{tag}")

            def f_rank(e, lg=lg):
                lgA = lg[:].rearrange("p (a c) -> p a c", a=1, c=8)
                lgB = lg[:].rearrange("p (a c) -> p a c", a=8, c=1)
                inA, inB = bass.broadcast_tensor_aps(lgA, lgB)
                gt3 = gtb[:].rearrange("p (a c) -> p a c", a=8, c=8)
                eq3 = eqb[:].rearrange("p (a c) -> p a c", a=8, c=8)
                lm3 = lmask[:].rearrange("p (a c) -> p a c", a=8, c=8)
                e.tensor_tensor(gt3, inA, inB, ALU.is_gt)
                e.tensor_tensor(eq3, inA, inB, ALU.is_equal)
                e.drain()
                e.tensor_tensor(eq3, eq3, lm3, ALU.mult)
                e.reduce_sum(rk[:], gt3, AX.X)
                e.drain()
                e.reduce_sum(cnt8[:], eq3, AX.X)
                e.drain()
                return e.tensor_tensor(rk[:], rk[:], cnt8[:], ALU.add)
            b.step("v", f_rank, reads=[lgk, "lmask"],
                   writes=["rk", "gtb", "eqb", "cnt8"], name=f"rank_{tag}")

            def f_cw(e, t_glob=t_glob):
                e.tensor_scalar(keep[:], rk[:], 2.0, None, ALU.is_lt)
                e.drain()
                return e.tensor_tensor(cw_sb[:, ts(t_glob, 8)], probs[:],
                                       keep[:], ALU.mult)
            b.step("v", f_cw, reads=["rk", "probs"], writes=["cw", "keep"],
                   name=f"cw_{tag}")

        transpose_tile(ogc[2], "ogc2", ogf, "ogf2", 2, f"{tag0}T2")
        tail_pass1(0)
        transpose_tile(ogc[0], "ogc0", ogf, "ogf3", 3, f"{tag0}T3")
        tail_pass1(1)
        tail_tr(0)
        tail_pass1(2)
        tail_tr(1)
        tail_rt(0)
        tail_pass1(3)
        tail_tr(2)
        tail_rt(1)
        tail_sm(0)
        tail_tr(3)
        tail_rt(2)
        tail_sm(1)
        tail_rt(3)
        tail_sm(2)
        tail_sm(3)

        xn2f_keys = [f"xn2f_{t}" for t in range(4)]
        if gi == 0:
            dbg_dump("ogf0", ogf[:], [128, 4096], BF16,
                     [f"ogf{t}" for t in range(4)])
            dbg_dump("xn2f0", xn2f[:], [128, 4096], BF16, xn2f_keys)
        src3 = xn2f[:].rearrange("p (k c2) -> p k c2", k=8, c2=512)
        chS.dma(lambda e, gi=gi, src3=src3: e.dma_start(
            dxn2d[:, :, ts(gi, 512)], src3),
            reads=xn2f_keys, writes=[f"xn2d{gi}"], name=f"xn2s_{tag0}")

    seq = [(gi, dxprev, False, "B") for gi in range(NG)]
    seq += [(gi, dx, True, "C") for gi in range(NG)]
    for tl in range(4):
        attn_ln_pre(0, dxprev, "B", tl)
        attn_ln_tr(0, "B", tl)
    done_c = phases.startswith("BC1")

    def mk_prefetch(i):
        if i + 1 >= len(seq):
            return None
        gi2, src2, full2, ph2 = seq[i + 1]

        def pf(stage, tl, gi2=gi2, src2=src2, ph2=ph2):
            if stage == 0:
                attn_ln_pre(gi2, src2, ph2, tl)
            else:
                attn_ln_tr(gi2, ph2, tl)
        return pf

    for i, (gi, src2, full2, ph2) in enumerate(seq):
        if ph2 == "C":
            break
        attn_group(gi, src2, full2, ph2, prefetch=mk_prefetch(i))
        if phases == "B1":
            break
    b.step("v", lambda e: e.tensor_copy(shist[:, ts(8, 512)], S_sb[:]),
           reads=["S"], writes=["sh8"], name="scinit")
    if "C" not in phases:
        chS.dma(lambda e: e.dma_start(dy[0:128, 0:512], S_sb[:]),
                reads=["S"], name="bisect_out")
        fin_deps0 = [DmaTok(sid, 16 * chS.count[i])
                     for i, sid in enumerate(chS.sem_ids) if chS.count[i] > 0]
        b.step("y", lambda e: e.nop(), deps=fin_deps0, name="fin0")
        b.resolve()
        with nc.Block() as block:
            @block.tensor
            def _(eng):
                b.emit(eng, "t")

            @block.vector
            def _(eng):
                b.emit(eng, "v")

            @block.scalar
            def _(eng):
                b.emit(eng, "s")

            @block.gpsimd
            def _(eng):
                b.emit(eng, "g")

            @block.sync
            def _(eng):
                b.emit(eng, "y")
        st_a.close()
        st_f.close()
        st_p.close()
        return nc, dbg_done
    if "S0" in dbg:
        dbg_dump("S0", S_sb[:], [128, 512], F32, ["S"])
    cut = phases[3] if (phases.startswith("BC1") and len(phases) > 3) else None
    for gi in range(NG):
        attn_group(gi, dx, True, "C", cut=cut,
                   prefetch=mk_prefetch(NG + gi))
        if phases.startswith("BC1"):
            break

    dbg_dump("cw", cw_sb[:], [128, 128], F32, ["cw"])
    dbg_dump("S1", S_sb[:], [128, 512], F32, ["S"])

    if "D" not in phases:
        chS.dma(lambda e: e.dma_start(dy[0:128, 0:1024],
                                      x2t[0][:]),
                reads=["x2t0"], name="bisect_out2")
        fin_deps1 = [DmaTok(sid, 16 * chS.count[i])
                     for i, sid in enumerate(chS.sem_ids) if chS.count[i] > 0]
        b.step("y", lambda e: e.nop(), deps=fin_deps1, name="fin1")
        b.resolve()
        with nc.Block() as block:
            @block.tensor
            def _(eng):
                b.emit(eng, "t")

            @block.vector
            def _(eng):
                b.emit(eng, "v")

            @block.scalar
            def _(eng):
                b.emit(eng, "s")

            @block.gpsimd
            def _(eng):
                b.emit(eng, "g")

            @block.sync
            def _(eng):
                b.emit(eng, "y")
        st_a.close()
        st_f.close()
        st_p.close()
        return nc, dbg_done
    b.barrier("c2d")
    st_a.close()

    w1b = sb(st_f, "w1b", [128, 16384], BF16)
    w2b = sb(st_f, "w2b", [128, 16384], BF16)
    hsb = sb(st_f, "hsb", [128, 16384], BF16)
    acc = sb(st_f, "acc", [128, 8192], F32)
    xn2h = sb(st_f, "xn2h", [128, 8192], BF16)
    x2l = [sb(st_f, f"x2l{i}", [128, 1024], F32) for i in range(2)]

    for q in range(4):
        def f_cwt(e, q=q):
            r = None
            for j in range(4):
                r = e.transpose(PR[0:8, ts(j, 128)],
                                cw_sb[:, ts(q * 4 + j, 8)], identf[:])
            return r
        b.step("t", f_cwt, reads=["cw", "identf"], writes=["PR"],
               name=f"cwt{q}")
        b.step("v", lambda e, q=q: e.tensor_copy(cwf[0:8, ts(q, 512)],
                                                 PR[0:8, :]),
               reads=["PR"], writes=["cwf"], name=f"cwe{q}")

    def load_w1(Hh, e_i):
        for k in range(8):
            chG.dma(lambda eng, e_i=e_i, k=k: eng.dma_start(
                w1b[:, ts(k, 2048)], dW1[e_i, ts(k, 128), :]),
                writes=[f"w1p{k}"], name=f"w1_{Hh}e{e_i}k{k}")

    def load_w2(Hh, e_i):
        for k in range(16):
            chG.dma(lambda eng, e_i=e_i, k=k: eng.dma_start(
                w2b[:, ts(k, 1024)], dW2[e_i, ts(k, 128), :]),
                writes=[f"w2p{k}"], name=f"w2_{Hh}e{e_i}k{k}")

    for Hh in range(2):
        tagH = f"D{Hh}"
        chY.dma(lambda e, Hh=Hh: e.dma_start(
            xn2h[:].rearrange("p (k c2) -> p k c2", k=8, c2=1024),
            dxn2d[:, :, ts(Hh, 1024)]),
            reads=[f"xn2d{g}" for g in range(NG)], writes=["xn2h"],
            name=f"xnl_{tagH}")
        for tl in range(8):
            t_glob = Hh * 8 + tl

            def f_b2(e, t_glob=t_glob):
                r = None
                for dc in range(2):
                    r = e.matmul(PB[4 + dc][:, :], cwf[:, ts(t_glob, 128)],
                                 b2sb[:, ts(dc, 512)], start=True, stop=True)
                return r
            b.step("t", f_b2, reads=["cwf", "b2sb"], writes=["PB4", "PB5"],
                   name=f"b2m_{tagH}t{tl}")

            def f_ai(e, tl=tl):
                e.tensor_copy(acc[:, sl(tl * 1024, 512)], PB[4][:, :])
                return e.tensor_copy(acc[:, sl(tl * 1024 + 512, 512)],
                                     PB[5][:, :])
            b.step("v", f_ai, reads=["PB4", "PB5"], writes=[f"acc{tl}"],
                   name=f"ai_{tagH}t{tl}")
        for tl in range(8):
            t_glob = Hh * 8 + tl
            chY.dma(lambda e, t_glob=t_glob, tl=tl: e.dma_start(
                x2l[tl % 2][:], dx2d[ts(t_glob, 128), :]),
                reads=[f"x2d{t_glob}"], writes=[f"x2l{tl%2}"],
                name=f"x2l_{tagH}t{tl}")

            def f_xadd(e, tl=tl):
                e.tensor_tensor(acc[:, sl(tl * 1024, 512)],
                                acc[:, sl(tl * 1024, 512)],
                                x2l[tl % 2][:, 0:512], ALU.add)
                return e.tensor_tensor(acc[:, sl(tl * 1024 + 512, 512)],
                                       acc[:, sl(tl * 1024 + 512, 512)],
                                       x2l[tl % 2][:, 512:1024], ALU.add)
            b.step("v", f_xadd, reads=[f"x2l{tl%2}"], writes=[f"acc{tl}"],
                   name=f"xadd_{tagH}t{tl}")

        if Hh == 0:
            load_w1(Hh, 0)
        for e_i in range(NE):
            tagE = f"{tagH}e{e_i}"
            load_w2(Hh, e_i)
            bank = 0
            for m in range(16):
                for tg in range(2):
                    pbi = bank % 6
                    bank += 1

                    def f_hm(e, m=m, tg=tg, pbi=pbi):
                        r = None
                        for k in range(8):
                            r = e.matmul(PB[pbi][:, :],
                                         w1b[:, sl(k * 2048 + m * 128, 128)],
                                         xn2h[:, sl(k * 1024 + tg * 512, 512)],
                                         start=(k == 0), stop=(k == 7))
                        return r
                    b.step("t", f_hm,
                           reads=[f"w1p{k}" for k in range(8)] + ["xn2h"],
                           writes=[f"PB{pbi}"], name=f"hm_{tagE}m{m}g{tg}")
                    b.step("s", lambda e, m=m, tg=tg, e_i=e_i, pbi=pbi:
                           e.activation(
                               hsb[:, sl(m * 1024 + tg * 512, 512)],
                               PB[pbi][:, :], AF.Relu,
                               bias=b1t[:, sl(e_i * 16 + m, 1)], scale=1.0),
                           reads=[f"PB{pbi}", "b1t"], writes=[f"h{m}"],
                           name=f"he_{tagE}m{m}g{tg}")
            if e_i + 1 < NE:
                load_w1(Hh, e_i + 1)
            elif Hh == 0:
                load_w1(1, 0)
            bank = 0
            for tl in range(8):
                t_glob = Hh * 8 + tl
                for dc in range(2):
                    pbi = bank % 6
                    bank += 1

                    def f_ym(e, tl=tl, dc=dc, pbi=pbi):
                        r = None
                        for m in range(16):
                            r = e.matmul(PB[pbi][:, :],
                                         hsb[:, sl(m * 1024 + tl * 128, 128)],
                                         w2b[:, sl(m * 1024 + dc * 512, 512)],
                                         start=(m == 0), stop=(m == 15))
                        return r
                    b.step("t", f_ym,
                           reads=[f"h{m}" for m in range(16)] +
                                 [f"w2p{k}" for k in range(16)],
                           writes=[f"PB{pbi}"], name=f"ym_{tagE}t{tl}d{dc}")

                    def f_ye(e, tl=tl, dc=dc, pbi=pbi, t_glob=t_glob, e_i=e_i):
                        a = acc[:, sl(tl * 1024 + dc * 512, 512)]
                        return e.scalar_tensor_tensor(
                            a, PB[pbi][:, :],
                            cw_sb[:, sl(t_glob * 8 + e_i, 1)], a,
                            ALU.mult, ALU.add)
                    b.step("v", f_ye, reads=[f"PB{pbi}", "cw"],
                           writes=[f"acc{tl}"], name=f"ye_{tagE}t{tl}d{dc}")
        for tl in range(8):
            t_glob = Hh * 8 + tl
            chS.dma(lambda e, t_glob=t_glob, tl=tl: e.dma_start(
                dy[ts(t_glob, 128), :], acc[:, ts(tl, 1024)]),
                reads=[f"acc{tl}"], name=f"out_{tagH}t{tl}")

    fin_deps = [DmaTok(sid, 16 * chS.count[i])
                for i, sid in enumerate(chS.sem_ids) if chS.count[i] > 0]
    b.step("y", lambda e: e.nop(), deps=fin_deps, name="fin")

    b.resolve()
    with nc.Block() as block:
        @block.tensor
        def _(eng):
            b.emit(eng, "t")

        @block.vector
        def _(eng):
            b.emit(eng, "v")

        @block.scalar
        def _(eng):
            b.emit(eng, "s")

        @block.gpsimd
        def _(eng):
            b.emit(eng, "g")

        @block.sync
        def _(eng):
            b.emit(eng, "y")

    st_f.close()
    st_p.close()
    return nc, dbg_done


def make_core_inputs(full_inputs, core):
    fi = full_inputs
    bb, half = core // 2, core % 2
    x = np.ascontiguousarray(np.asarray(fi["L0_attn_x"])[bb])
    xc = np.ascontiguousarray(x[half * T:(half + 1) * T])
    xp = (np.ascontiguousarray(x[0:T]) if half == 1
          else np.zeros((T, D), dtype=np.float32))
    f32 = lambda a: np.ascontiguousarray(np.asarray(a, dtype=np.float32))
    return {
        "x": xc,
        "xprev": xp,
        "Wq": f32(fi["L0_attn_Wq"]),
        "Wk": f32(fi["L0_attn_Wk"]),
        "Wv": f32(fi["L0_attn_Wv"]),
        "Wg": f32(fi["L0_attn_Wg"]),
        "Wgk1": f32(fi["L0_attn_Wgk1"]),
        "Wgk2": f32(fi["L0_attn_Wgk2"]),
        "bgk2t": f32(fi["L0_attn_bgk2"]).reshape(4, 128).T.copy(),
        "bgk2r": f32(fi["L0_attn_bgk2"]).reshape(1, 512),
        "normw": f32(fi["L0_attn_norm_w"]).reshape(128, 1),
        "Wo": f32(fi["L0_attn_Wo"]),
        "Wr": f32(fi["L0_ffn_Wr"]),
        "br": f32(fi["L0_ffn_br"]).reshape(1, 8),
        "W1": f32(fi["L0_ffn_W1"]),
        "b1t": np.ascontiguousarray(
            f32(fi["L0_ffn_b1"]).reshape(8, 16, 128).transpose(2, 0, 1).reshape(128, 128)),
        "W2": f32(fi["L0_ffn_W2"]),
        "b2": f32(fi["L0_ffn_b2"]),
    }




_NC_CACHE = {}


def kernel(**inputs):
    from concourse.bass_utils import run_bass_kernel_spmd
    if "nc" not in _NC_CACHE:
        _NC_CACHE["nc"], _ = build_core_program(dbg=())
    nc = _NC_CACHE["nc"]
    in_maps = [make_core_inputs(inputs, c) for c in range(8)]
    res = run_bass_kernel_spmd(nc, in_maps, core_ids=list(range(8)))
    out = np.empty((4, 4096, 1024), dtype=np.float32)
    for c in range(8):
        bb, half = c // 2, c % 2
        out[bb, half * T:(half + 1) * T] = res.results[c]["y"]
    return out
